# version 02
# baseline (speedup 1.0000x reference)
import numpy as np
import ml_dtypes
import concourse.bass as bass
import concourse.mybir as mybir
from concourse.bass_utils import run_bass_kernel_spmd
from contextlib import ExitStack

F32 = mybir.dt.float32
BF16 = mybir.dt.bfloat16
AF = mybir.ActivationFunctionType
ALU = mybir.AluOpType
AXL = mybir.AxisListType
BF = ml_dtypes.bfloat16

NT = 4096
DM = 1024
DI = 2048
DS = 16
DR = 64
NE = 8
TILE = 512
NTL = NT // TILE
FT = 1024
NFT = NT // FT
NCH_M = DM // 128
NCH_I = DI // 128
NQ = FT // 128


class Build:
    def __init__(self, debug=False):
        self.nc = bass.Bass("TRN2", target_bir_lowering=False, debug=False)
        self.debug = debug
        self.cnt = {}
        self.sem = {}
        self.prev_mark = None
        self.OPS = None
        self.bar_n = 0

    def _op(self, eng, fn):
        self.OPS[eng].append(fn)

    def _inc(self, name, n=1):
        self.cnt[name] = self.cnt.get(name, 0) + n
        return self.cnt[name]

    def step(self, eng, fn, also_wait=()):
        pm = self.prev_mark
        if pm is not None:
            s, v = self.sem[pm[0]], pm[1]
            self._op(eng, lambda E, s=s, v=v: E.wait_ge(s, v))
        for (wn, wv) in also_wait:
            s = self.sem[wn]
            self._op(eng, lambda E, s=s, wv=wv: E.wait_ge(s, wv))
        mv = self._inc("mk_" + eng)
        s = self.sem["mk_" + eng]

        def run(E, fn=fn, s=s):
            inst = fn(E)
            assert inst is not None, "step emitter must return its last instruction"
            inst.then_inc(s, 1)
        self._op(eng, run)
        self.prev_mark = ("mk_" + eng, mv)
        return self.prev_mark

    def dstep(self, queue, pairs):
        qn = "q_" + queue
        pm = self.prev_mark
        if pm is not None:
            s, v = self.sem[pm[0]], pm[1]
            self._op(queue, lambda E, s=s, v=v: E.wait_ge(s, v))
        qs = self.sem[qn]
        for (o, i) in pairs:
            self._inc(qn, 16)
            self._op(queue, lambda E, o=o, i=i, qs=qs:
                     E.dma_start(out=o, in_=i).then_inc(qs, 16))
        self.prev_mark = (qn, self.cnt[qn])
        return self.prev_mark

    def stage(self, body):
        self.OPS = {k: [] for k in ["sync", "gpsimd", "tensor", "vector", "scalar"]}
        body()
        totals = [(n, self.cnt.get(n, 0)) for n in
                  ["mk_sync", "mk_gpsimd", "mk_tensor", "mk_vector",
                   "mk_scalar", "q_sync", "q_gpsimd"]]
        for k in self.OPS:
            for (n, v) in totals:
                if v:
                    s = self.sem[n]
                    self._op(k, lambda E, s=s, v=v: E.wait_ge(s, v))
        OPS = self.OPS
        with self.nc.Block() as blk:
            blk.sync(lambda E: [f(E) for f in OPS["sync"]])
            blk.gpsimd(lambda E: [f(E) for f in OPS["gpsimd"]])
            blk.tensor(lambda E: [f(E) for f in OPS["tensor"]])
            blk.vector(lambda E: [f(E) for f in OPS["vector"]])
            blk.scalar(lambda E: [f(E) for f in OPS["scalar"]])
        self.OPS = None


def build(debug=False, max_stage=9):
    B = Build(debug)
    nc = B.nc

    dram_in = lambda n, s, d=F32: nc.dram_tensor(n, list(s), d, kind="ExternalInput").ap()
    dram_out = lambda n, s, d=F32: nc.dram_tensor(n, list(s), d, kind="ExternalOutput").ap()
    dram_tmp = lambda n, s, d=F32: nc.dram_tensor(n, list(s), d, kind="Internal").ap()

    X = dram_in("x_fm", (DM, NT))
    W_IN = dram_in("w_in", (DM, 2 * DI))
    B_IN = dram_in("b_in_l", (128, 2 * NCH_I))
    W_OUT = dram_in("w_out", (DI, DM))
    B_OUT = dram_in("b_out_l", (128, NCH_M))
    W_DTIN = dram_in("w_dtin", (DI, DR))
    B_DTIN = dram_in("b_dtin_l", (DR, 1))
    W_DTOUT = dram_in("w_dtout", (DR, DI))
    B_DTSUM = dram_in("b_dtsum_l", (128, NCH_I))
    W_B = dram_in("w_b", (DI, DS))
    B_B = dram_in("b_b_l", (DS, 1))
    W_C = dram_in("w_c", (DI, DS))
    B_C = dram_in("b_c_l", (DS, 1))
    CONVW = dram_in("convw_l", (128, NCH_I * 4))
    CONVB = dram_in("convb_l", (128, NCH_I))
    ALOG = dram_in("alog_l", (128, NCH_I * DS))
    DVEC = dram_in("d_l", (128, NCH_I))
    NSC = dram_in("nsc_l", (128, NCH_M))
    GATE = dram_in("gate_wT", (DM, NE))
    W1 = dram_in("ffn_w1", (NE * DM, DM))
    W2 = dram_in("ffn_w2", (NE * DM, DM))
    W3 = dram_in("ffn_w3", (NE * DM, DM))
    B1 = dram_in("ffn_b1_l", (128, NE * NCH_M))
    B3 = dram_in("ffn_b3_l", (128, NE * NCH_M))
    B2R = dram_in("ffn_b2_row", (1, NE * DM))
    SW1 = dram_in("ffn_sw1", (DM, DM))
    SW2 = dram_in("ffn_sw2", (DM, DM))
    SW3 = dram_in("ffn_sw3", (DM, DM))
    SB1 = dram_in("ffn_sb1_l", (128, NCH_M))
    SB3 = dram_in("ffn_sb3_l", (128, NCH_M))
    SB2R = dram_in("ffn_sb2_row", (1, DM))

    OUT = dram_out("out_fm", (DM, NT))
    if debug:
        D_CFM = dram_out("d_cfm", (1, NE * NT), BF16)
        D_YAC = dram_out("d_yac", (DM, NT), F32)

    if debug:
        dram_tmp = lambda n, s, d=F32: nc.dram_tensor(n, list(s), d,
                                                      kind="ExternalOutput").ap()
    S_XN = dram_tmp("scr_xn", (DM, NT), F32)
    S_XNBF = dram_tmp("scr_xnbf", (DM, NT), BF16)
    S_XN2BF = dram_tmp("scr_xn2bf", (DM, NT), BF16)
    S_UBF = dram_tmp("scr_ubf", (DI, NT), BF16)
    S_ZBF = dram_tmp("scr_zbf", (DI, NT), BF16)
    S_DTBF = dram_tmp("scr_dtbf", (DI, NT), BF16)
    S_B = dram_tmp("scr_b", (DS, NT), F32)
    S_C = dram_tmp("scr_c", (DS, NT), F32)
    S_YBF = dram_tmp("scr_ybf", (DI, NT), BF16)
    S_S = dram_tmp("scr_s", (DM, NT), F32)


    with ExitStack() as ctx:
        e = ctx.enter_context

        c_bin = e(nc.sbuf_tensor("c_bin", [128, 2 * NCH_I], F32))
        c_bout = e(nc.sbuf_tensor("c_bout", [128, NCH_M], F32))
        c_wdtin = e(nc.sbuf_tensor("c_wdtin", [128, NCH_I * DR], BF16))
        c_bdtin = e(nc.sbuf_tensor("c_bdtin", [DR, 1], F32))
        c_wdtout = e(nc.sbuf_tensor("c_wdtout", [DR, DI], BF16))
        c_bdtsum = e(nc.sbuf_tensor("c_bdtsum", [128, NCH_I], F32))
        c_wb = e(nc.sbuf_tensor("c_wb", [128, NCH_I * DS], BF16))
        c_bb = e(nc.sbuf_tensor("c_bb", [DS, 1], F32))
        c_wc = e(nc.sbuf_tensor("c_wc", [128, NCH_I * DS], BF16))
        c_bc = e(nc.sbuf_tensor("c_bc", [DS, 1], F32))
        c_convw = e(nc.sbuf_tensor("c_convw", [128, NCH_I * 4], F32))
        c_convb = e(nc.sbuf_tensor("c_convb", [128, NCH_I], F32))
        c_a = e(nc.sbuf_tensor("c_a", [128, NCH_I * DS], F32))
        c_d = e(nc.sbuf_tensor("c_d", [128, NCH_I], F32))
        c_nsc = e(nc.sbuf_tensor("c_nsc", [128, NCH_M], F32))
        c_gate = e(nc.sbuf_tensor("c_gate", [128, NCH_M * NE], F32))
        c_b1 = e(nc.sbuf_tensor("c_b1", [128, NE * NCH_M], F32))
        c_b3 = e(nc.sbuf_tensor("c_b3", [128, NE * NCH_M], F32))
        c_b2row = e(nc.sbuf_tensor("c_b2row", [1, NE * DM], BF16))
        c_sb1 = e(nc.sbuf_tensor("c_sb1", [128, NCH_M], F32))
        c_sb3 = e(nc.sbuf_tensor("c_sb3", [128, NCH_M], F32))
        c_sb2row = e(nc.sbuf_tensor("c_sb2row", [1, DM], BF16))
        c_ones_colb = e(nc.sbuf_tensor("c_ones_colb", [128, 1], BF16))
        c_ones_f32 = e(nc.sbuf_tensor("c_ones_f32", [1, 128], F32))
        c_onesr_t = e(nc.sbuf_tensor("c_onesr_t", [1, FT], BF16))
        sc_state = e(nc.sbuf_tensor("sc_state", [128, NCH_I * DS], F32))
        cv_hist = e(nc.sbuf_tensor("cv_hist", [128, NCH_I * 4], F32))

        ident_dram = nc.inline_tensor(np.eye(128, dtype=BF), "ident_bf")
        c_ident = e(nc.sbuf_tensor("c_ident", [128, 128], BF16))

        pb = [e(nc.psum_tensor(f"pb{i}", [128, 512], F32)) for i in range(8)]

        for name in ["q_sync", "q_gpsimd", "mk_sync", "mk_gpsimd", "mk_tensor",
                     "mk_vector", "mk_scalar", "bar"]:
            B.sem[name] = e(nc.semaphore(name))

        st = B.step
        dst = B.dstep

        def p0():
            dst("sync", [
                (c_bin[:], B_IN), (c_bout[:], B_OUT), (c_bdtin[:], B_DTIN),
                (c_bdtsum[:], B_DTSUM), (c_bb[:], B_B), (c_bc[:], B_C),
                (c_convw[:], CONVW), (c_convb[:], CONVB),
                (c_a[:], ALOG), (c_d[:], DVEC), (c_nsc[:], NSC),
                (c_gate[:].rearrange("p (c e) -> p c e", e=NE),
                 GATE.rearrange("(c p) e -> p c e", p=128)),
                (c_b1[:], B1), (c_b3[:], B3),
                (c_sb1[:], SB1), (c_sb3[:], SB3),
            ])
            dst("sync", [(c_ident[:], ident_dram.ap())])
            dst("gpsimd", [
                (c_wdtin[:].rearrange("p (c r) -> p c r", r=DR),
                 W_DTIN.rearrange("(c p) r -> p c r", p=128)),
                (c_wdtout[:], W_DTOUT),
                (c_wb[:].rearrange("p (c s) -> p c s", s=DS),
                 W_B.rearrange("(c p) s -> p c s", p=128)),
                (c_wc[:].rearrange("p (c s) -> p c s", s=DS),
                 W_C.rearrange("(c p) s -> p c s", p=128)),
                (c_b2row[:], B2R), (c_sb2row[:], SB2R),
            ])
            st("vector", lambda v: v.memset(sc_state[:], 0.0))
            st("vector", lambda v: v.memset(cv_hist[:], 0.0))
            st("vector", lambda v: v.memset(c_ones_colb[:], 1.0))
            st("vector", lambda v: v.memset(c_ones_f32[:], 1.0))
            st("vector", lambda v: v.memset(c_onesr_t[:], 1.0))
            st("scalar", lambda s: s.activation(c_a[:], c_a[:], AF.Exp))
            st("vector", lambda v: v.tensor_scalar_mul(c_a[:], c_a[:], -1.0))
        B.stage(p0)

        def emit_rsqrt(v_row, out_row, t1, t2, eps):
            st("vector", lambda v: v.tensor_scalar_add(t1, v_row, eps))
            st("vector", lambda v: v.tensor_scalar(
                t2.bitcast(mybir.dt.int32), t1.bitcast(mybir.dt.int32),
                1, None, ALU.logical_shift_right))
            st("vector", lambda v: v.tensor_scalar(
                t2.bitcast(mybir.dt.int32), t2.bitcast(mybir.dt.int32),
                -1, 0x5f3759df, ALU.mult, ALU.add))
            for _ in range(3):
                st("vector", lambda v: v.tensor_tensor(out_row, t2, t2, ALU.mult))
                st("vector", lambda v: v.tensor_tensor(out_row, out_row, t1, ALU.mult))
                st("vector", lambda v: v.tensor_scalar(out_row, out_row, -0.5, 1.5,
                                                       ALU.mult, ALU.add))
                st("vector", lambda v: v.tensor_tensor(t2, t2, out_row, ALU.mult))
            st("vector", lambda v: v.tensor_copy(out_row, t2))

        def emit_rep(row_ap, out_tile, pbi, n):
            st("tensor", lambda t: t.matmul(pb[pbi][:, 0:n], c_ones_f32[0:1, :],
                                            row_ap, start=True, stop=True))
            st("vector", lambda v: v.tensor_copy(out_tile, pb[pbi][:, 0:n]))

        def p1():
            x_t = e1(nc.sbuf_tensor("x_t", [128, NCH_M * TILE], F32))
            xb_t = e1(nc.sbuf_tensor("xb_t", [128, NCH_M * TILE], BF16))
            xn_t = e1(nc.sbuf_tensor("xn_t", [128, NCH_M * TILE], F32))
            xnb_t = e1(nc.sbuf_tensor("xnb_t", [128, NCH_M * TILE], BF16))
            xn2_t = e1(nc.sbuf_tensor("xn2_t", [128, NCH_M * TILE], BF16))
            sq_t = e1(nc.sbuf_tensor("sq_t", [128, TILE], BF16))
            rep_a = e1(nc.sbuf_tensor("rep_a", [128, TILE], F32))
            rep_b = e1(nc.sbuf_tensor("rep_b", [128, TILE], F32))
            srow = e1(nc.sbuf_tensor("srow", [1, 8 * TILE], F32))
            r_m = srow[:, 0 * TILE:1 * TILE]
            r_msq = srow[:, 1 * TILE:2 * TILE]
            r_v = srow[:, 2 * TILE:3 * TILE]
            r_rstd = srow[:, 3 * TILE:4 * TILE]
            r_t1 = srow[:, 4 * TILE:5 * TILE]
            r_t2 = srow[:, 5 * TILE:6 * TILE]

            for it in range(NTL):
                t0 = it * TILE
                dst("sync", [(x_t[:, c * TILE:(c + 1) * TILE],
                              X[c * 128:(c + 1) * 128, t0:t0 + TILE])
                             for c in range(NCH_M)])
                st("vector", lambda v: v.tensor_copy(xb_t[:], x_t[:]))
                def mm_mean(t):
                    for c in range(NCH_M):
                        i = t.matmul(pb[0][0:1, 0:TILE], c_ones_colb[:],
                                     xb_t[:, c * TILE:(c + 1) * TILE],
                                     start=(c == 0), stop=(c == NCH_M - 1))
                    return i
                st("tensor", mm_mean)
                for c in range(NCH_M):
                    st("scalar", lambda s, c=c: s.activation(
                        sq_t[:], x_t[:, c * TILE:(c + 1) * TILE], AF.Square))
                    st("tensor", lambda t, c=c: t.matmul(
                        pb[1][0:1, 0:TILE], c_ones_colb[:], sq_t[:],
                        start=(c == 0), stop=(c == NCH_M - 1)))
                st("scalar", lambda s: s.activation(r_m, pb[0][0:1, 0:TILE],
                                                    AF.Identity, scale=1.0 / DM))
                st("scalar", lambda s: s.activation(r_msq, pb[1][0:1, 0:TILE],
                                                    AF.Identity, scale=1.0 / DM))
                st("vector", lambda v: v.tensor_tensor(r_v, r_m, r_m, ALU.mult))
                st("vector", lambda v: v.tensor_tensor(r_v, r_msq, r_v, ALU.subtract))
                emit_rsqrt(r_v, r_rstd, r_t1, r_t2, 1e-5)
                emit_rep(r_m, rep_a[:], 2, TILE)
                emit_rep(r_rstd, rep_b[:], 3, TILE)
                def v_xn(v):
                    for c in range(NCH_M):
                        sl = slice(c * TILE, (c + 1) * TILE)
                        v.tensor_tensor(xn_t[:, sl], x_t[:, sl], rep_a[:], ALU.subtract)
                        v.tensor_tensor(xn_t[:, sl], xn_t[:, sl], rep_b[:], ALU.mult)
                        i = v.tensor_copy(xnb_t[:, sl], xn_t[:, sl])
                    return i
                st("vector", v_xn)
                for c in range(NCH_M):
                    st("scalar", lambda s, c=c: s.activation(
                        sq_t[:], xn_t[:, c * TILE:(c + 1) * TILE], AF.Square))
                    st("tensor", lambda t, c=c: t.matmul(
                        pb[4][0:1, 0:TILE], c_ones_colb[:], sq_t[:],
                        start=(c == 0), stop=(c == NCH_M - 1)))
                st("scalar", lambda s: s.activation(r_msq, pb[4][0:1, 0:TILE],
                                                    AF.Identity, scale=1.0 / DM))
                emit_rsqrt(r_msq, r_rstd, r_t1, r_t2, 1e-6)
                emit_rep(r_rstd, rep_a[:], 5, TILE)
                def v_xn2(v):
                    for c in range(NCH_M):
                        sl = slice(c * TILE, (c + 1) * TILE)
                        i = v.scalar_tensor_tensor(xn2_t[:, sl], xn_t[:, sl],
                                                   c_nsc[:, c:c + 1], rep_a[:],
                                                   ALU.mult, ALU.mult)
                    return i
                st("vector", v_xn2)
                dst("sync",
                    [(S_XN[c * 128:(c + 1) * 128, t0:t0 + TILE],
                      xn_t[:, c * TILE:(c + 1) * TILE]) for c in range(NCH_M)] +
                    [(S_XNBF[c * 128:(c + 1) * 128, t0:t0 + TILE],
                      xnb_t[:, c * TILE:(c + 1) * TILE]) for c in range(NCH_M)] +
                    [(S_XN2BF[c * 128:(c + 1) * 128, t0:t0 + TILE],
                      xn2_t[:, c * TILE:(c + 1) * TILE]) for c in range(NCH_M)])

        if max_stage >= 1:
            with ExitStack() as s1:
                e1 = s1.enter_context
                B.stage(p1)

        def p2():
            w_win = e1(nc.sbuf_tensor("w_win", [128, NCH_M * 2 * DI], BF16))
            xn2_t = e1(nc.sbuf_tensor("p2xn2", [128, NCH_M * TILE], BF16))
            u_t = e1(nc.sbuf_tensor("p2u", [128, NCH_I * TILE], BF16))
            z_t = e1(nc.sbuf_tensor("p2z", [128, NCH_I * TILE], BF16))
            uext = e1(nc.sbuf_tensor("p2uext", [128, TILE + 4], F32))
            uc_f = e1(nc.sbuf_tensor("p2uc", [128, TILE], F32))
            ub_f = e1(nc.sbuf_tensor("p2ub", [128, TILE], F32))
            tt_t = e1(nc.sbuf_tensor("p2tt", [128, TILE], F32))
            t2_t = e1(nc.sbuf_tensor("p2t2", [128, TILE], F32))

            dst("gpsimd", [(w_win[:].rearrange("p (c n) -> p c n", c=NCH_M),
                            W_IN.rearrange("(c p) n -> p c n", p=128))])
            for it in range(NTL):
                t0 = it * TILE
                dst("sync", [(xn2_t[:, c * TILE:(c + 1) * TILE],
                              S_XN2BF[c * 128:(c + 1) * 128, t0:t0 + TILE])
                             for c in range(NCH_M)])
                for j in range(2 * NCH_I):
                    def mm(t, j=j):
                        for k in range(NCH_M):
                            i = t.matmul(pb[0][:, 0:TILE],
                                         w_win[:, k * 2 * DI + j * 128:
                                               k * 2 * DI + (j + 1) * 128],
                                         xn2_t[:, k * TILE:(k + 1) * TILE],
                                         start=(k == 0), stop=(k == NCH_M - 1))
                        return i
                    st("tensor", mm)
                    if j < NCH_I:
                        st("scalar", lambda s, j=j: s.activation(
                            uext[:, 3:3 + TILE], pb[0][:, 0:TILE], AF.Identity,
                            bias=c_bin[:, j:j + 1]))
                        def v_conv(v, j=j):
                            v.tensor_copy(uext[:, 0:3], cv_hist[:, j * 4:j * 4 + 3])
                            v.tensor_scalar_mul(uc_f[:], uext[:, 0:TILE],
                                                c_convw[:, j * 4:j * 4 + 1])
                            for k in range(1, 4):
                                v.scalar_tensor_tensor(
                                    uc_f[:], uext[:, k:k + TILE],
                                    c_convw[:, j * 4 + k:j * 4 + k + 1],
                                    uc_f[:], ALU.mult, ALU.add)
                            return v.tensor_copy(cv_hist[:, j * 4:j * 4 + 3],
                                                 uext[:, TILE:TILE + 3])
                        st("vector", v_conv)
                        def s_silu(s, j=j):
                            s.activation(ub_f[:], uc_f[:], AF.Identity,
                                         bias=c_convb[:, j:j + 1])
                            return s.activation(tt_t[:], ub_f[:], AF.Tanh, scale=0.5)
                        st("scalar", s_silu)
                        def v_fin(v, j=j):
                            v.tensor_scalar(t2_t[:], tt_t[:], 1.0, 0.5,
                                            ALU.add, ALU.mult)
                            return v.tensor_tensor(u_t[:, j * TILE:(j + 1) * TILE],
                                                   ub_f[:], t2_t[:], ALU.mult)
                        st("vector", v_fin)
                    else:
                        jz = j - NCH_I
                        st("scalar", lambda s, j=j: s.activation(
                            ub_f[:], pb[0][:, 0:TILE], AF.Identity,
                            bias=c_bin[:, j:j + 1]))
                        st("scalar", lambda s: s.activation(
                            tt_t[:], ub_f[:], AF.Tanh, scale=0.5))
                        def v_finz(v, jz=jz):
                            v.tensor_scalar(t2_t[:], tt_t[:], 1.0, 0.5,
                                            ALU.add, ALU.mult)
                            return v.tensor_tensor(z_t[:, jz * TILE:(jz + 1) * TILE],
                                                   ub_f[:], t2_t[:], ALU.mult)
                        st("vector", v_finz)
                dst("sync",
                    [(S_UBF[c * 128:(c + 1) * 128, t0:t0 + TILE],
                      u_t[:, c * TILE:(c + 1) * TILE]) for c in range(NCH_I)] +
                    [(S_ZBF[c * 128:(c + 1) * 128, t0:t0 + TILE],
                      z_t[:, c * TILE:(c + 1) * TILE]) for c in range(NCH_I)])

        if max_stage >= 2:
            with ExitStack() as s1:
                e1 = s1.enter_context
                B.stage(p2)

        def p3():
            u_t = e1(nc.sbuf_tensor("p3u", [128, NCH_I * TILE], BF16))
            dtr_t = e1(nc.sbuf_tensor("p3dtr", [DR, TILE], BF16))
            dt_t = e1(nc.sbuf_tensor("p3dt", [128, NCH_I * TILE], BF16))
            yy = e1(nc.sbuf_tensor("p3y", [128, TILE], F32))
            q1 = e1(nc.sbuf_tensor("p3q1", [128, TILE], F32))
            ba_t = e1(nc.sbuf_tensor("p3b", [DS, TILE], F32))
            ca_t = e1(nc.sbuf_tensor("p3c", [DS, TILE], F32))

            for it in range(NTL):
                t0 = it * TILE
                dst("sync", [(u_t[:, c * TILE:(c + 1) * TILE],
                              S_UBF[c * 128:(c + 1) * 128, t0:t0 + TILE])
                             for c in range(NCH_I)])
                def mm_dtr(t):
                    for k in range(NCH_I):
                        i = t.matmul(pb[0][0:DR, 0:TILE],
                                     c_wdtin[:, k * DR:(k + 1) * DR],
                                     u_t[:, k * TILE:(k + 1) * TILE],
                                     start=(k == 0), stop=(k == NCH_I - 1))
                    return i
                st("tensor", mm_dtr)
                st("scalar", lambda s: s.activation(dtr_t[:], pb[0][0:DR, 0:TILE],
                                                    AF.Identity,
                                                    bias=c_bdtin[:, 0:1]))
                def mm_b(t):
                    for k in range(NCH_I):
                        i = t.matmul(pb[1][0:DS, 0:TILE],
                                     c_wb[:, k * DS:(k + 1) * DS],
                                     u_t[:, k * TILE:(k + 1) * TILE],
                                     start=(k == 0), stop=(k == NCH_I - 1))
                    return i
                st("tensor", mm_b)
                st("scalar", lambda s: s.activation(ba_t[:], pb[1][0:DS, 0:TILE],
                                                    AF.Tanh, bias=c_bb[:, 0:1]))
                def mm_c(t):
                    for k in range(NCH_I):
                        i = t.matmul(pb[2][0:DS, 0:TILE],
                                     c_wc[:, k * DS:(k + 1) * DS],
                                     u_t[:, k * TILE:(k + 1) * TILE],
                                     start=(k == 0), stop=(k == NCH_I - 1))
                    return i
                st("tensor", mm_c)
                st("scalar", lambda s: s.activation(ca_t[:], pb[2][0:DS, 0:TILE],
                                                    AF.Tanh, bias=c_bc[:, 0:1]))
                dst("sync", [(S_B[:, t0:t0 + TILE], ba_t[:]),
                             (S_C[:, t0:t0 + TILE], ca_t[:])])
                for j in range(NCH_I):
                    st("tensor", lambda t, j=j: t.matmul(
                        pb[3][:, 0:TILE], c_wdtout[:, j * 128:(j + 1) * 128],
                        dtr_t[:], start=True, stop=True))
                    st("scalar", lambda s, j=j: s.activation(
                        yy[:], pb[3][:, 0:TILE], AF.Exp,
                        bias=c_bdtsum[:, j:j + 1]))
                    def v_dt(v, j=j):
                        v.tensor_scalar(q1[:], yy[:], 1.0 / 3.0, -0.5,
                                        ALU.mult, ALU.add)
                        v.tensor_tensor(q1[:], q1[:], yy[:], ALU.mult)
                        v.tensor_scalar_add(q1[:], q1[:], 1.0)
                        v.tensor_tensor(q1[:], q1[:], yy[:], ALU.mult)
                        return v.tensor_scalar(dt_t[:, j * TILE:(j + 1) * TILE],
                                               q1[:], 1e-3, 0.1, ALU.max, ALU.min)
                    st("vector", v_dt)
                dst("sync", [(S_DTBF[c * 128:(c + 1) * 128, t0:t0 + TILE],
                              dt_t[:, c * TILE:(c + 1) * TILE])
                             for c in range(NCH_I)])

        if max_stage >= 3:
            with ExitStack() as s1:
                e1 = s1.enter_context
                B.stage(p3)

        def p4():
            u_t = e1(nc.sbuf_tensor("p4u", [128, NCH_I * TILE], BF16))
            z_t = e1(nc.sbuf_tensor("p4z", [128, NCH_I * TILE], BF16))
            dt_t = e1(nc.sbuf_tensor("p4dt", [128, NCH_I * TILE], BF16))
            y_t = e1(nc.sbuf_tensor("p4y", [128, NCH_I * TILE], BF16))
            ba_t = e1(nc.sbuf_tensor("p4b", [DS, TILE], F32))
            ca_t = e1(nc.sbuf_tensor("p4c", [DS, TILE], F32))
            brep = e1(nc.sbuf_tensor("p4brep", [128, 8 * TILE], BF16))
            crep = e1(nc.sbuf_tensor("p4crep", [128, 8 * TILE], BF16))
            da_t = e1(nc.sbuf_tensor("p4da", [128, 8 * TILE], F32))
            db_t = e1(nc.sbuf_tensor("p4db", [128, 8 * TILE], BF16))
            h_t = e1(nc.sbuf_tensor("p4h", [128, 8 * TILE], BF16))
            m_t = e1(nc.sbuf_tensor("p4m", [128, 8 * TILE], BF16))
            tr4 = e1(nc.sbuf_tensor("p4tr4", [128, 4 * TILE], BF16))
            dtu_t = e1(nc.sbuf_tensor("p4dtu", [128, TILE], BF16))
            ysal = e1(nc.sbuf_tensor("p4ysal", [128, NCH_I * TILE], F32))
            yh_t = e1(nc.sbuf_tensor("p4yh", [128, TILE], F32))

            for it in range(NTL):
                t0 = it * TILE
                dst("sync",
                    [(u_t[:, c * TILE:(c + 1) * TILE],
                      S_UBF[c * 128:(c + 1) * 128, t0:t0 + TILE]) for c in range(NCH_I)] +
                    [(z_t[:, c * TILE:(c + 1) * TILE],
                      S_ZBF[c * 128:(c + 1) * 128, t0:t0 + TILE]) for c in range(NCH_I)] +
                    [(dt_t[:, c * TILE:(c + 1) * TILE],
                      S_DTBF[c * 128:(c + 1) * 128, t0:t0 + TILE]) for c in range(NCH_I)] +
                    [(ba_t[:], S_B[:, t0:t0 + TILE]),
                     (ca_t[:], S_C[:, t0:t0 + TILE])])
                for hf in range(2):
                    dst("gpsimd", [(brep[0:1, :], ba_t[hf * 8:hf * 8 + 8, :]),
                                   (crep[0:1, :], ca_t[hf * 8:hf * 8 + 8, :])])
                    for kd in range(7):
                        dst("sync", [(brep[1 << kd:2 << kd, :], brep[0:1 << kd, :]),
                                     (crep[1 << kd:2 << kd, :], crep[0:1 << kd, :])])
                    for c in range(NCH_I):
                        st("vector", lambda v, c=c: v.tensor_tensor(
                            dtu_t[:], dt_t[:, c * TILE:(c + 1) * TILE],
                            u_t[:, c * TILE:(c + 1) * TILE], ALU.mult))
                        def s_da(s, c=c, hf=hf):
                            for sl in range(8):
                                i = s.activation(da_t[:, sl * TILE:(sl + 1) * TILE],
                                                 dt_t[:, c * TILE:(c + 1) * TILE],
                                                 AF.Exp,
                                                 scale=c_a[:, c * DS + hf * 8 + sl:
                                                           c * DS + hf * 8 + sl + 1])
                            return i
                        st("scalar", s_da)
                        b3v = brep[:].rearrange("p (s t) -> p s t", s=8)
                        db3 = db_t[:].rearrange("p (s t) -> p s t", s=8)
                        u3 = dtu_t[:].unsqueeze(1).broadcast_to([128, 8, TILE])
                        st("vector", lambda v, b3v=b3v, db3=db3, u3=u3:
                           v.tensor_tensor(db3, b3v, u3, ALU.mult))
                        def v_scan(v, c=c, hf=hf):
                            for sl in range(8):
                                v.tensor_tensor_scan(
                                    h_t[:, sl * TILE:(sl + 1) * TILE],
                                    da_t[:, sl * TILE:(sl + 1) * TILE],
                                    db_t[:, sl * TILE:(sl + 1) * TILE],
                                    sc_state[:, c * DS + hf * 8 + sl:
                                             c * DS + hf * 8 + sl + 1],
                                    ALU.mult, ALU.add)
                            v.tensor_copy(
                                sc_state[:, c * DS + hf * 8:c * DS + hf * 8 + 8],
                                h_t[:].rearrange("p (s t) -> p s t", s=8)[:, :, TILE - 1])
                            v.tensor_tensor(m_t[:], h_t[:], crep[:], ALU.mult)
                            v.tensor_tensor(tr4[:], m_t[:, 0:4 * TILE],
                                            m_t[:, 4 * TILE:8 * TILE], ALU.add)
                            v.tensor_tensor(tr4[:, 0:2 * TILE], tr4[:, 0:2 * TILE],
                                            tr4[:, 2 * TILE:4 * TILE], ALU.add)
                            ys_c = ysal[:, c * TILE:(c + 1) * TILE]
                            if hf == 0:
                                i = v.tensor_tensor(ys_c, tr4[:, 0:TILE],
                                                    tr4[:, TILE:2 * TILE], ALU.add)
                            else:
                                v.tensor_tensor(yh_t[:], tr4[:, 0:TILE],
                                                tr4[:, TILE:2 * TILE], ALU.add)
                                i = v.tensor_tensor(ys_c, ys_c, yh_t[:], ALU.add)
                            return i
                        st("vector", v_scan)
                for c in range(NCH_I):
                    def v_y(v, c=c):
                        v.scalar_tensor_tensor(yh_t[:],
                                               u_t[:, c * TILE:(c + 1) * TILE],
                                               c_d[:, c:c + 1],
                                               ysal[:, c * TILE:(c + 1) * TILE],
                                               ALU.mult, ALU.add)
                        return v.tensor_tensor(y_t[:, c * TILE:(c + 1) * TILE],
                                               yh_t[:],
                                               z_t[:, c * TILE:(c + 1) * TILE],
                                               ALU.mult)
                    st("vector", v_y)
                dst("sync", [(S_YBF[c * 128:(c + 1) * 128, t0:t0 + TILE],
                              y_t[:, c * TILE:(c + 1) * TILE])
                             for c in range(NCH_I)])

        if max_stage >= 4:
            with ExitStack() as s1:
                e1 = s1.enter_context
                B.stage(p4)

        def p5():
            w_wo = e1(nc.sbuf_tensor("p5wo", [128, NCH_I * DM], BF16))
            y_t = e1(nc.sbuf_tensor("p5y", [128, NCH_I * TILE], BF16))
            xn_t = e1(nc.sbuf_tensor("p5xn", [128, NCH_M * TILE], F32))
            s_t = e1(nc.sbuf_tensor("p5s", [128, NCH_M * TILE], F32))

            dst("gpsimd", [(w_wo[:].rearrange("p (c n) -> p c n", c=NCH_I),
                            W_OUT.rearrange("(c p) n -> p c n", p=128))])
            for it in range(NTL):
                t0 = it * TILE
                dst("sync",
                    [(y_t[:, c * TILE:(c + 1) * TILE],
                      S_YBF[c * 128:(c + 1) * 128, t0:t0 + TILE]) for c in range(NCH_I)] +
                    [(xn_t[:, c * TILE:(c + 1) * TILE],
                      S_XN[c * 128:(c + 1) * 128, t0:t0 + TILE]) for c in range(NCH_M)])
                for j in range(NCH_M):
                    def mm(t, j=j):
                        for k in range(NCH_I):
                            i = t.matmul(pb[0][:, 0:TILE],
                                         w_wo[:, k * DM + j * 128:k * DM + (j + 1) * 128],
                                         y_t[:, k * TILE:(k + 1) * TILE],
                                         start=(k == 0), stop=(k == NCH_I - 1))
                        return i
                    st("tensor", mm)
                    st("vector", lambda v, j=j: v.scalar_tensor_tensor(
                        s_t[:, j * TILE:(j + 1) * TILE], pb[0][:, 0:TILE],
                        c_bout[:, j:j + 1], xn_t[:, j * TILE:(j + 1) * TILE],
                        ALU.add, ALU.add))
                dst("sync", [(S_S[c * 128:(c + 1) * 128, t0:t0 + TILE],
                              s_t[:, c * TILE:(c + 1) * TILE])
                             for c in range(NCH_M)])

        if max_stage >= 5:
            with ExitStack() as s1:
                e1 = s1.enter_context
                B.stage(p5)

        def p6():
            w_r = e1(nc.sbuf_tensor("p6wr", [128, 3 * NCH_M * DM], BF16))
            xnb_t = e1(nc.sbuf_tensor("p6xnb", [128, NCH_M * FT], BF16))
            hb_t = e1(nc.sbuf_tensor("p6hb", [128, NCH_M * FT], BF16))
            yac_t = e1(nc.sbuf_tensor("p6yac", [128, NCH_M * FT], F32))
            p1b = e1(nc.sbuf_tensor("p6p1b", [128, FT], BF16))
            p3b = e1(nc.sbuf_tensor("p6p3b", [128, FT], BF16))
            th_t = e1(nc.sbuf_tensor("p6th", [128, FT], BF16))
            t2_t = e1(nc.sbuf_tensor("p6t2", [128, FT], BF16))
            crepe = e1(nc.sbuf_tensor("p6crepe", [128, FT], BF16))
            cfm = e1(nc.sbuf_tensor("p6cfm", [1, NE * FT], BF16))
            gq = e1(nc.sbuf_tensor("p6gq", [128, 6 * NE], F32))
            gq8 = e1(nc.sbuf_tensor("p6gq8", [128, NE], BF16))
            xc_t = e1(nc.sbuf_tensor("p6xc", [128, FT], F32))
            sc_c = e1(nc.sbuf_tensor("p6sc", [128, FT], F32))
            oc_t = e1(nc.sbuf_tensor("p6oc", [128, FT], F32))
            g1r = gq[:, 0 * NE:1 * NE]
            g2r = gq[:, 1 * NE:2 * NE]
            g3r = gq[:, 2 * NE:3 * NE]
            g4r = gq[:, 3 * NE:4 * NE]
            g5r = gq[:, 4 * NE:5 * NE]
            g6r = gq[:, 5 * NE:6 * NE]

            w1s = w_r[:, 0 * NCH_M * DM:1 * NCH_M * DM]
            w3s = w_r[:, 1 * NCH_M * DM:2 * NCH_M * DM]
            w2s = w_r[:, 2 * NCH_M * DM:3 * NCH_M * DM]

            for ft in range(NFT):
                t0 = ft * FT
                dst("sync",
                    [(xnb_t[:, c * FT:(c + 1) * FT],
                      S_XNBF[c * 128:(c + 1) * 128, t0:t0 + FT]) for c in range(NCH_M)])
                for k in range(NCH_M):
                    dst("sync", [(xc_t[:], S_XN[k * 128:(k + 1) * 128, t0:t0 + FT])])
                    def mm_g(t, k=k):
                        for q in range(NQ):
                            i = t.matmul(pb[q][:, 0:NE],
                                         xc_t[:, q * 128:(q + 1) * 128],
                                         c_gate[:, k * NE:(k + 1) * NE],
                                         start=(k == 0), stop=(k == NCH_M - 1))
                        return i
                    st("tensor", mm_g)
                for q in range(NQ):
                    st("vector", lambda v, q=q: v.tensor_copy(
                        g1r, pb[q][:, 0:NE]))
                    st("vector", lambda v: v.reduce_max(g5r[:, 0:1], g1r, axis=AXL.X))
                    st("vector", lambda v: v.tensor_scalar_mul(g6r[:, 0:1],
                                                               g5r[:, 0:1], -1.0))
                    st("scalar", lambda s: s.activation(g2r, g1r, AF.Exp,
                                                        bias=g6r[:, 0:1]))
                    st("vector", lambda v: v.reduce_sum(g5r[:, 1:2], g2r, axis=AXL.X))
                    st("vector", lambda v: v.reciprocal(g6r[:, 1:2], g5r[:, 1:2]))
                    st("vector", lambda v: v.tensor_scalar_mul(g2r, g2r, g6r[:, 1:2]))
                    st("vector", lambda v: v.max(g3r, g1r))
                    st("vector", lambda v: v.tensor_scalar(g4r, g1r, g3r[:, 1:2],
                                                           None, ALU.is_ge))
                    st("vector", lambda v: v.tensor_tensor(g2r, g2r, g4r, ALU.mult))
                    st("vector", lambda v: v.tensor_copy(gq8[:], g2r))
                    st("tensor", lambda t, q=q: t.transpose(
                        pb[q][0:NE, 0:64].bitcast(BF16), gq8[:], c_ident[:]))
                    st("scalar", lambda s, q=q: s.activation(
                        th_t[0:NE, 0:128], pb[q][0:NE, 0:64].bitcast(BF16),
                        AF.Identity))
                    dst("sync", [(cfm[0:1, :].rearrange("o (e t) -> o e t", e=NE)
                                  [:, :, q * 128:(q + 1) * 128],
                                  th_t[0:NE, 0:128])])
                for ex in range(NE + 1):
                    shared = (ex == NE)
                    if shared:
                        pairs = [(w1s.rearrange("p (c n) -> p c n", c=NCH_M),
                                  SW1.rearrange("(c p) n -> p c n", p=128)),
                                 (w3s.rearrange("p (c n) -> p c n", c=NCH_M),
                                  SW3.rearrange("(c p) n -> p c n", p=128)),
                                 (w2s.rearrange("p (c n) -> p c n", c=NCH_M),
                                  SW2.rearrange("(c p) n -> p c n", p=128))]
                    else:
                        pairs = [(w1s.rearrange("p (c n) -> p c n", c=NCH_M),
                                  W1[ex * DM:(ex + 1) * DM, :]
                                  .rearrange("(c p) n -> p c n", p=128)),
                                 (w3s.rearrange("p (c n) -> p c n", c=NCH_M),
                                  W3[ex * DM:(ex + 1) * DM, :]
                                  .rearrange("(c p) n -> p c n", p=128)),
                                 (w2s.rearrange("p (c n) -> p c n", c=NCH_M),
                                  W2[ex * DM:(ex + 1) * DM, :]
                                  .rearrange("(c p) n -> p c n", p=128))]
                    dst("gpsimd", pairs)
                    if not shared:
                        def mm_rep2(t, ex=ex):
                            t.matmul(pb[6][:, 0:512], c_onesr_t[0:1, 0:128],
                                     cfm[0:1, ex * FT:ex * FT + 512],
                                     start=True, stop=True)
                            return t.matmul(pb[7][:, 0:512], c_onesr_t[0:1, 0:128],
                                            cfm[0:1, ex * FT + 512:ex * FT + 1024],
                                            start=True, stop=True)
                        st("tensor", mm_rep2)
                        st("scalar", lambda s: s.activation(
                            crepe[:, 0:512], pb[6][:, 0:512], AF.Identity))
                        st("scalar", lambda s: s.activation(
                            crepe[:, 512:1024], pb[7][:, 0:512], AF.Identity))
                    b1c = c_sb1 if shared else c_b1[:, ex * NCH_M:(ex + 1) * NCH_M]
                    b3c = c_sb3 if shared else c_b3[:, ex * NCH_M:(ex + 1) * NCH_M]
                    for j in range(NCH_M):
                        def mm_p13(t, j=j):
                            for hh in range(2):
                                for k in range(NCH_M):
                                    t.matmul(pb[0 + hh][:, 0:512],
                                             w1s[:, k * DM + j * 128:k * DM + (j + 1) * 128],
                                             xnb_t[:, k * FT + hh * 512:k * FT + (hh + 1) * 512],
                                             start=(k == 0), stop=(k == NCH_M - 1))
                                for k in range(NCH_M):
                                    i = t.matmul(pb[2 + hh][:, 0:512],
                                                 w3s[:, k * DM + j * 128:k * DM + (j + 1) * 128],
                                                 xnb_t[:, k * FT + hh * 512:k * FT + (hh + 1) * 512],
                                                 start=(k == 0), stop=(k == NCH_M - 1))
                            return i
                        st("tensor", mm_p13)
                        def s_ev(s, j=j, b1c=b1c, b3c=b3c):
                            for hh in range(2):
                                s.activation(p1b[:, hh * 512:(hh + 1) * 512],
                                             pb[0 + hh][:, 0:512], AF.Identity,
                                             bias=b1c[:, j:j + 1])
                                s.activation(p3b[:, hh * 512:(hh + 1) * 512],
                                             pb[2 + hh][:, 0:512], AF.Identity,
                                             bias=b3c[:, j:j + 1])
                            return s.activation(th_t[:], p1b[:], AF.Tanh, scale=0.5)
                        st("scalar", s_ev)
                        def v_h(v, j=j, shared=shared):
                            v.tensor_scalar(t2_t[:], th_t[:], 1.0, 0.5,
                                            ALU.add, ALU.mult)
                            v.tensor_tensor(t2_t[:], t2_t[:], p1b[:], ALU.mult)
                            v.tensor_tensor(t2_t[:], t2_t[:], p3b[:], ALU.mult)
                            if shared:
                                return v.tensor_copy(hb_t[:, j * FT:(j + 1) * FT],
                                                     t2_t[:])
                            return v.tensor_tensor(hb_t[:, j * FT:(j + 1) * FT],
                                                   t2_t[:], crepe[:], ALU.mult)
                        st("vector", v_h)
                    for j2 in range(NCH_M):
                        def mm_w2(t, j2=j2, ex=ex, shared=shared):
                            for hh in range(2):
                                for k in range(NCH_M):
                                    t.matmul(pb[4 + hh][:, 0:512],
                                             w2s[:, k * DM + j2 * 128:k * DM + (j2 + 1) * 128],
                                             hb_t[:, k * FT + hh * 512:k * FT + (hh + 1) * 512],
                                             start=(k == 0), stop=False)
                                brow = (c_sb2row[0:1, j2 * 128:(j2 + 1) * 128]
                                        if shared else
                                        c_b2row[0:1, ex * DM + j2 * 128:ex * DM + (j2 + 1) * 128])
                                mrow = (c_onesr_t[0:1, hh * 512:(hh + 1) * 512]
                                        if shared else
                                        cfm[0:1, ex * FT + hh * 512:ex * FT + (hh + 1) * 512])
                                i = t.matmul(pb[4 + hh][:, 0:512], brow, mrow,
                                             start=False, stop=True)
                            return i
                        st("tensor", mm_w2)
                        def v_acc(v, j2=j2, ex=ex):
                            for hh in range(2):
                                dst_sl = yac_t[:, j2 * FT + hh * 512:j2 * FT + (hh + 1) * 512]
                                if ex == 0:
                                    i = v.tensor_copy(dst_sl, pb[4 + hh][:, 0:512])
                                else:
                                    i = v.tensor_tensor(dst_sl, dst_sl,
                                                        pb[4 + hh][:, 0:512], ALU.add)
                            return i
                        st("vector", v_acc)
                if B.debug:
                    dst("sync", [(D_CFM[0:1, NE * t0:NE * (t0 + FT)], cfm[:])] +
                        [(D_YAC[j2 * 128:(j2 + 1) * 128, t0:t0 + FT],
                          yac_t[:, j2 * FT:(j2 + 1) * FT]) for j2 in range(NCH_M)])
                for j2 in range(NCH_M):
                    dst("sync", [(xc_t[:], X[j2 * 128:(j2 + 1) * 128, t0:t0 + FT]),
                                 (sc_c[:], S_S[j2 * 128:(j2 + 1) * 128, t0:t0 + FT])])
                    st("scalar", lambda s, j2=j2: s.activation(
                        th_t[:], yac_t[:, j2 * FT:(j2 + 1) * FT], AF.Tanh, scale=0.5))
                    def v_out(v, j2=j2):
                        v.tensor_scalar(t2_t[:], th_t[:], 1.0, 0.5, ALU.add, ALU.mult)
                        v.tensor_tensor(oc_t[:], yac_t[:, j2 * FT:(j2 + 1) * FT],
                                        t2_t[:], ALU.mult)
                        v.tensor_tensor(oc_t[:], oc_t[:], sc_c[:], ALU.mult)
                        return v.tensor_tensor(oc_t[:], oc_t[:], xc_t[:], ALU.add)
                    st("vector", v_out)
                    dst("sync", [(OUT[j2 * 128:(j2 + 1) * 128, t0:t0 + FT], oc_t[:])])

        if max_stage >= 6:
            with ExitStack() as s1:
                e1 = s1.enter_context
                B.stage(p6)

        def pf():
            st("sync", lambda E: E.nop())
        B.stage(pf)

    return B



def _layout_inputs(inputs):
    g = lambda k: np.asarray(inputs[k])
    out = {}
    x = g("L0_attn_x")
    xT = np.ascontiguousarray(x.transpose(0, 2, 1))

    def chunk_layout(vec, nch):
        return np.ascontiguousarray(np.asarray(vec).reshape(nch, 128).T)

    out["w_in"] = np.ascontiguousarray(g("L0_attn_in_proj_w"))
    out["b_in_l"] = chunk_layout(g("L0_attn_in_proj_b"), 2 * NCH_I)
    out["w_out"] = np.ascontiguousarray(g("L0_attn_out_proj_w"))
    out["b_out_l"] = chunk_layout(g("L0_attn_out_proj_b"), NCH_M)
    out["w_dtin"] = np.ascontiguousarray(g("L0_attn_dt_in_w"))
    out["b_dtin_l"] = np.ascontiguousarray(g("L0_attn_dt_in_b").reshape(DR, 1))
    out["w_dtout"] = np.ascontiguousarray(g("L0_attn_dt_out_w"))
    out["b_dtsum_l"] = chunk_layout(
        g("L0_attn_dt_out_b") + g("L0_attn_dt_bias"), NCH_I)
    out["w_b"] = np.ascontiguousarray(g("L0_attn_b_proj_w"))
    out["b_b_l"] = np.ascontiguousarray(g("L0_attn_b_proj_b").reshape(DS, 1))
    out["w_c"] = np.ascontiguousarray(g("L0_attn_c_proj_w"))
    out["b_c_l"] = np.ascontiguousarray(g("L0_attn_c_proj_b").reshape(DS, 1))
    ck = g("L0_attn_conv_kernel")
    out["convw_l"] = np.ascontiguousarray(
        ck.T.reshape(NCH_I, 128, 4).transpose(1, 0, 2).reshape(128, NCH_I * 4))
    out["convb_l"] = chunk_layout(g("L0_attn_conv_bias"), NCH_I)
    al = g("L0_attn_a_log")
    out["alog_l"] = np.ascontiguousarray(
        al.reshape(NCH_I, 128, DS).transpose(1, 0, 2).reshape(128, NCH_I * DS))
    out["d_l"] = chunk_layout(g("L0_attn_d"), NCH_I)
    out["nsc_l"] = chunk_layout(g("L0_attn_norm_scale"), NCH_M)
    out["gate_wT"] = np.ascontiguousarray(g("L0_ffn_gate_w").T)
    out["ffn_w1"] = np.ascontiguousarray(g("L0_ffn_w1").reshape(NE * DM, DM))
    out["ffn_w2"] = np.ascontiguousarray(g("L0_ffn_w2").reshape(NE * DM, DM))
    out["ffn_w3"] = np.ascontiguousarray(g("L0_ffn_w3").reshape(NE * DM, DM))
    b1 = g("L0_ffn_b1")
    out["ffn_b1_l"] = np.ascontiguousarray(
        b1.reshape(NE, NCH_M, 128).transpose(2, 0, 1).reshape(128, NE * NCH_M))
    b3 = g("L0_ffn_b3")
    out["ffn_b3_l"] = np.ascontiguousarray(
        b3.reshape(NE, NCH_M, 128).transpose(2, 0, 1).reshape(128, NE * NCH_M))
    out["ffn_b2_row"] = np.ascontiguousarray(g("L0_ffn_b2").reshape(1, NE * DM))
    out["ffn_sw1"] = np.ascontiguousarray(g("L0_ffn_sw1"))
    out["ffn_sw2"] = np.ascontiguousarray(g("L0_ffn_sw2"))
    out["ffn_sw3"] = np.ascontiguousarray(g("L0_ffn_sw3"))
    out["ffn_sb1_l"] = chunk_layout(g("L0_ffn_sb1"), NCH_M)
    out["ffn_sb3_l"] = chunk_layout(g("L0_ffn_sb3"), NCH_M)
    out["ffn_sb2_row"] = np.ascontiguousarray(g("L0_ffn_sb2").reshape(1, DM))
    return xT, out


_NC_CACHE = {}


def kernel(**inputs):
    xT, shared = _layout_inputs(inputs)
    if "nc" not in _NC_CACHE:
        _NC_CACHE["nc"] = build(debug=False)
    Bb = _NC_CACHE["nc"]
    in_maps = []
    for b in range(8):
        m = dict(shared)
        m["x_fm"] = np.ascontiguousarray(xT[b])
        in_maps.append(m)
    res = run_bass_kernel_spmd(Bb.nc, in_maps, core_ids=list(range(8)))
    outs = np.stack([np.asarray(r["out_fm"]).T for r in res.results], axis=0)
    return np.ascontiguousarray(outs.astype(np.float32))


# version 03
# speedup vs baseline: 1.5252x; 1.5252x over previous
import os
import numpy as np
import ml_dtypes
import concourse.bass as bass
import concourse.mybir as mybir
from concourse.bass_utils import run_bass_kernel_spmd
from contextlib import ExitStack

F32 = mybir.dt.float32
BF16 = mybir.dt.bfloat16
AF = mybir.ActivationFunctionType
ALU = mybir.AluOpType
AXL = mybir.AxisListType
BF = ml_dtypes.bfloat16

NT = 4096
DM = 1024
DI = 2048
DS = 16
DR = 64
NE = 8
TILE = 512
NTL = NT // TILE
FT = 1024
NFT = NT // FT
NCH_M = DM // 128
NCH_I = DI // 128
NQ = FT // 128


class Build:
    def __init__(self, debug=False):
        self.nc = bass.Bass("TRN2", target_bir_lowering=False, debug=False)
        self.debug = debug
        self.cnt = {}
        self.sem = {}
        self.prev_mark = None
        self.OPS = None
        self.bar_n = 0

    def _op(self, eng, fn):
        self.OPS[eng].append(fn)

    def _inc(self, name, n=1):
        self.cnt[name] = self.cnt.get(name, 0) + n
        return self.cnt[name]

    def step(self, eng, fn, also_wait=(), after=None):
        deps = [self.prev_mark] if after is None else list(after)
        seen = {}
        for d in deps:
            if d is None:
                continue
            seen[d[0]] = max(seen.get(d[0], 0), d[1])
        for nm, v in seen.items():
            if v > 0:
                s = self.sem[nm]
                self._op(eng, lambda E, s=s, v=v: E.wait_ge(s, v))
        for (wn, wv) in also_wait:
            s = self.sem[wn]
            self._op(eng, lambda E, s=s, wv=wv: E.wait_ge(s, wv))
        mv = self._inc("mk_" + eng)
        s = self.sem["mk_" + eng]

        def run(E, fn=fn, s=s):
            inst = fn(E)
            assert inst is not None, "step emitter must return its last instruction"
            inst.then_inc(s, 1)
        self._op(eng, run)
        self.prev_mark = ("mk_" + eng, mv)
        return self.prev_mark

    def dstep(self, queue, pairs, after=None):
        qn = "q_" + queue
        deps = [self.prev_mark] if after is None else list(after)
        seen = {}
        for d in deps:
            if d is None:
                continue
            seen[d[0]] = max(seen.get(d[0], 0), d[1])
        for nm, v in seen.items():
            if v > 0:
                s = self.sem[nm]
                self._op(queue, lambda E, s=s, v=v: E.wait_ge(s, v))
        qs = self.sem[qn]
        for (o, i) in pairs:
            self._inc(qn, 16)
            self._op(queue, lambda E, o=o, i=i, qs=qs:
                     E.dma_start(out=o, in_=i).then_inc(qs, 16))
        self.prev_mark = (qn, self.cnt[qn])
        return self.prev_mark

    def stage(self, body):
        self.OPS = {k: [] for k in ["sync", "gpsimd", "tensor", "vector", "scalar"]}
        body()
        totals = [(n, self.cnt.get(n, 0)) for n in
                  ["mk_sync", "mk_gpsimd", "mk_tensor", "mk_vector",
                   "mk_scalar", "q_sync", "q_gpsimd"]]
        for k in self.OPS:
            for (n, v) in totals:
                if v:
                    s = self.sem[n]
                    self._op(k, lambda E, s=s, v=v: E.wait_ge(s, v))
        OPS = self.OPS
        with self.nc.Block() as blk:
            blk.sync(lambda E: [f(E) for f in OPS["sync"]])
            blk.gpsimd(lambda E: [f(E) for f in OPS["gpsimd"]])
            blk.tensor(lambda E: [f(E) for f in OPS["tensor"]])
            blk.vector(lambda E: [f(E) for f in OPS["vector"]])
            blk.scalar(lambda E: [f(E) for f in OPS["scalar"]])
        self.OPS = None


def build(debug=False, max_stage=9):
    B = Build(debug)
    nc = B.nc

    dram_in = lambda n, s, d=F32: nc.dram_tensor(n, list(s), d, kind="ExternalInput").ap()
    dram_out = lambda n, s, d=F32: nc.dram_tensor(n, list(s), d, kind="ExternalOutput").ap()
    dram_tmp = lambda n, s, d=F32: nc.dram_tensor(n, list(s), d, kind="Internal").ap()

    X = dram_in("x_fm", (DM, NT))
    W_IN = dram_in("w_in", (DM, 2 * DI))
    B_IN = dram_in("b_in_l", (128, 2 * NCH_I))
    W_OUT = dram_in("w_out", (DI, DM))
    B_OUT = dram_in("b_out_l", (128, NCH_M))
    W_DTIN = dram_in("w_dtin", (DI, DR))
    B_DTIN = dram_in("b_dtin_l", (DR, 1))
    W_DTOUT = dram_in("w_dtout", (DR, DI))
    B_DTSUM = dram_in("b_dtsum_l", (128, NCH_I))
    W_B = dram_in("w_b", (DI, DS))
    B_B = dram_in("b_b_l", (DS, 1))
    W_C = dram_in("w_c", (DI, DS))
    B_C = dram_in("b_c_l", (DS, 1))
    CONVW = dram_in("convw_l", (128, NCH_I * 4))
    CONVB = dram_in("convb_l", (128, NCH_I))
    ALOG = dram_in("alog_l", (128, NCH_I * DS))
    DVEC = dram_in("d_l", (128, NCH_I))
    NSC = dram_in("nsc_l", (128, NCH_M))
    GATE = dram_in("gate_wT", (DM, NE))
    W1 = dram_in("ffn_w1", (NE * DM, DM))
    W2 = dram_in("ffn_w2", (NE * DM, DM))
    W3 = dram_in("ffn_w3", (NE * DM, DM))
    B1 = dram_in("ffn_b1_l", (128, NE * NCH_M))
    B3 = dram_in("ffn_b3_l", (128, NE * NCH_M))
    B2AUG = dram_in("ffn_b2aug", (128, DM))
    SW1 = dram_in("ffn_sw1", (DM, DM))
    SW2 = dram_in("ffn_sw2", (DM, DM))
    SW3 = dram_in("ffn_sw3", (DM, DM))
    SB1 = dram_in("ffn_sb1_l", (128, NCH_M))
    SB3 = dram_in("ffn_sb3_l", (128, NCH_M))

    OUT = dram_out("out_fm", (DM, NT))
    if debug:
        D_CFM = dram_out("d_cfm", (DS, NT), BF16)
        D_YAC = dram_out("d_yac", (DM, NT), F32)

    if debug:
        dram_tmp = lambda n, s, d=F32: nc.dram_tensor(n, list(s), d,
                                                      kind="ExternalOutput").ap()
    S_XN = dram_tmp("scr_xn", (DM, NT), F32)
    S_XNBF = dram_tmp("scr_xnbf", (DM, NT), BF16)
    S_XN2BF = dram_tmp("scr_xn2bf", (DM, NT), BF16)
    S_UBF = dram_tmp("scr_ubf", (DI, NT), BF16)
    S_ZBF = dram_tmp("scr_zbf", (DI, NT), BF16)
    S_DTBF = dram_tmp("scr_dtbf", (DI, NT), BF16)
    S_B = dram_tmp("scr_b", (DS, NT), F32)
    S_C = dram_tmp("scr_c", (DS, NT), F32)
    S_YBF = dram_tmp("scr_ybf", (DI, NT), BF16)
    S_S = dram_tmp("scr_s", (DM, NT), F32)


    with ExitStack() as ctx:
        e = ctx.enter_context

        c_bin = e(nc.sbuf_tensor("c_bin", [128, 2 * NCH_I], F32))
        c_bout = e(nc.sbuf_tensor("c_bout", [128, NCH_M], F32))
        c_wdtin = e(nc.sbuf_tensor("c_wdtin", [128, NCH_I * DR], BF16))
        c_bdtin = e(nc.sbuf_tensor("c_bdtin", [DR, 1], F32))
        c_wdtout = e(nc.sbuf_tensor("c_wdtout", [DR, DI], BF16))
        c_bdtsum = e(nc.sbuf_tensor("c_bdtsum", [128, NCH_I], F32))
        c_wb = e(nc.sbuf_tensor("c_wb", [128, NCH_I * DS], BF16))
        c_bb = e(nc.sbuf_tensor("c_bb", [DS, 1], F32))
        c_wc = e(nc.sbuf_tensor("c_wc", [128, NCH_I * DS], BF16))
        c_bc = e(nc.sbuf_tensor("c_bc", [DS, 1], F32))
        c_convw = e(nc.sbuf_tensor("c_convw", [128, NCH_I * 4], F32))
        c_convb = e(nc.sbuf_tensor("c_convb", [128, NCH_I], F32))
        c_a = e(nc.sbuf_tensor("c_a", [128, NCH_I * DS], F32))
        c_d = e(nc.sbuf_tensor("c_d", [128, NCH_I], F32))
        c_nsc = e(nc.sbuf_tensor("c_nsc", [128, NCH_M], F32))
        c_gate = e(nc.sbuf_tensor("c_gate", [128, NCH_M * NE], F32))
        c_b1 = e(nc.sbuf_tensor("c_b1", [128, NE * NCH_M], F32))
        c_b3 = e(nc.sbuf_tensor("c_b3", [128, NE * NCH_M], F32))
        c_b2aug = e(nc.sbuf_tensor("c_b2aug", [128, DM], BF16))
        c_sb1 = e(nc.sbuf_tensor("c_sb1", [128, NCH_M], F32))
        c_sb3 = e(nc.sbuf_tensor("c_sb3", [128, NCH_M], F32))
        c_ones_colb = e(nc.sbuf_tensor("c_ones_colb", [128, 1], BF16))
        c_ones_f32 = e(nc.sbuf_tensor("c_ones_f32", [1, 128], F32))
        sc_state = e(nc.sbuf_tensor("sc_state", [128, NCH_I * DS], F32))
        cv_hist = e(nc.sbuf_tensor("cv_hist", [128, NCH_I * 4], F32))

        ident_dram = nc.inline_tensor(np.eye(128, dtype=BF), "ident_bf")
        onespad_np = np.zeros((128 - NE, FT), dtype=BF)
        onespad_np[0, :] = 1.0
        onespad_dram = nc.inline_tensor(onespad_np, "onespad_bf")
        c_ident = e(nc.sbuf_tensor("c_ident", [128, 128], BF16))

        pb = [e(nc.psum_tensor(f"pb{i}", [128, 512], F32)) for i in range(8)]

        for name in ["q_sync", "q_gpsimd", "mk_sync", "mk_gpsimd", "mk_tensor",
                     "mk_vector", "mk_scalar", "bar"]:
            B.sem[name] = e(nc.semaphore(name))

        st = B.step
        dst = B.dstep

        def p0():
            dst("sync", [
                (c_bin[:], B_IN), (c_bout[:], B_OUT), (c_bdtin[:], B_DTIN),
                (c_bdtsum[:], B_DTSUM), (c_bb[:], B_B), (c_bc[:], B_C),
                (c_convw[:], CONVW), (c_convb[:], CONVB),
                (c_a[:], ALOG), (c_d[:], DVEC), (c_nsc[:], NSC),
                (c_gate[:].rearrange("p (c e) -> p c e", e=NE),
                 GATE.rearrange("(c p) e -> p c e", p=128)),
                (c_b1[:], B1), (c_b3[:], B3),
                (c_sb1[:], SB1), (c_sb3[:], SB3),
            ])
            dst("sync", [(c_ident[:], ident_dram.ap())])
            dst("gpsimd", [
                (c_wdtin[:].rearrange("p (c r) -> p c r", r=DR),
                 W_DTIN.rearrange("(c p) r -> p c r", p=128)),
                (c_wdtout[:], W_DTOUT),
                (c_wb[:].rearrange("p (c s) -> p c s", s=DS),
                 W_B.rearrange("(c p) s -> p c s", p=128)),
                (c_wc[:].rearrange("p (c s) -> p c s", s=DS),
                 W_C.rearrange("(c p) s -> p c s", p=128)),
                (c_b2aug[:], B2AUG),
            ])
            st("vector", lambda v: v.memset(sc_state[:], 0.0))
            st("vector", lambda v: v.memset(cv_hist[:], 0.0))
            st("vector", lambda v: v.memset(c_ones_colb[:], 1.0))
            st("vector", lambda v: v.memset(c_ones_f32[:], 1.0))
            st("scalar", lambda s: s.activation(c_a[:], c_a[:], AF.Exp))
            st("vector", lambda v: v.tensor_scalar_mul(c_a[:], c_a[:], -1.0))
        B.stage(p0)

        def emit_rsqrt(v_row, out_row, t1, t2, eps):
            st("vector", lambda v: v.tensor_scalar_add(t1, v_row, eps))
            st("vector", lambda v: v.tensor_scalar(
                t2.bitcast(mybir.dt.int32), t1.bitcast(mybir.dt.int32),
                1, None, ALU.logical_shift_right))
            st("vector", lambda v: v.tensor_scalar(
                t2.bitcast(mybir.dt.int32), t2.bitcast(mybir.dt.int32),
                -1, 0x5f3759df, ALU.mult, ALU.add))
            for _ in range(3):
                st("vector", lambda v: v.tensor_tensor(out_row, t2, t2, ALU.mult))
                st("vector", lambda v: v.tensor_tensor(out_row, out_row, t1, ALU.mult))
                st("vector", lambda v: v.tensor_scalar(out_row, out_row, -0.5, 1.5,
                                                       ALU.mult, ALU.add))
                st("vector", lambda v: v.tensor_tensor(t2, t2, out_row, ALU.mult))
            st("vector", lambda v: v.tensor_copy(out_row, t2))

        def emit_rep(row_ap, out_tile, pbi, n):
            st("tensor", lambda t: t.matmul(pb[pbi][:, 0:n], c_ones_f32[0:1, :],
                                            row_ap, start=True, stop=True))
            st("vector", lambda v: v.tensor_copy(out_tile, pb[pbi][:, 0:n]))

        def p1():
            x_t = e1(nc.sbuf_tensor("x_t", [128, NCH_M * TILE], F32))
            xb_t = e1(nc.sbuf_tensor("xb_t", [128, NCH_M * TILE], BF16))
            xn_t = e1(nc.sbuf_tensor("xn_t", [128, NCH_M * TILE], F32))
            xnb_t = e1(nc.sbuf_tensor("xnb_t", [128, NCH_M * TILE], BF16))
            xn2_t = e1(nc.sbuf_tensor("xn2_t", [128, NCH_M * TILE], BF16))
            sq_t = e1(nc.sbuf_tensor("sq_t", [128, TILE], BF16))
            rep_a = e1(nc.sbuf_tensor("rep_a", [128, TILE], F32))
            rep_b = e1(nc.sbuf_tensor("rep_b", [128, TILE], F32))
            srow = e1(nc.sbuf_tensor("srow", [1, 8 * TILE], F32))
            r_m = srow[:, 0 * TILE:1 * TILE]
            r_msq = srow[:, 1 * TILE:2 * TILE]
            r_v = srow[:, 2 * TILE:3 * TILE]
            r_rstd = srow[:, 3 * TILE:4 * TILE]
            r_t1 = srow[:, 4 * TILE:5 * TILE]
            r_t2 = srow[:, 5 * TILE:6 * TILE]

            for it in range(NTL):
                t0 = it * TILE
                dst("sync", [(x_t[:, c * TILE:(c + 1) * TILE],
                              X[c * 128:(c + 1) * 128, t0:t0 + TILE])
                             for c in range(NCH_M)])
                st("vector", lambda v: v.tensor_copy(xb_t[:], x_t[:]))
                def mm_mean(t):
                    for c in range(NCH_M):
                        i = t.matmul(pb[0][0:1, 0:TILE], c_ones_colb[:],
                                     xb_t[:, c * TILE:(c + 1) * TILE],
                                     start=(c == 0), stop=(c == NCH_M - 1))
                    return i
                st("tensor", mm_mean)
                for c in range(NCH_M):
                    st("scalar", lambda s, c=c: s.activation(
                        sq_t[:], x_t[:, c * TILE:(c + 1) * TILE], AF.Square))
                    st("tensor", lambda t, c=c: t.matmul(
                        pb[1][0:1, 0:TILE], c_ones_colb[:], sq_t[:],
                        start=(c == 0), stop=(c == NCH_M - 1)))
                st("scalar", lambda s: s.activation(r_m, pb[0][0:1, 0:TILE],
                                                    AF.Identity, scale=1.0 / DM))
                st("scalar", lambda s: s.activation(r_msq, pb[1][0:1, 0:TILE],
                                                    AF.Identity, scale=1.0 / DM))
                st("vector", lambda v: v.tensor_tensor(r_v, r_m, r_m, ALU.mult))
                st("vector", lambda v: v.tensor_tensor(r_v, r_msq, r_v, ALU.subtract))
                emit_rsqrt(r_v, r_rstd, r_t1, r_t2, 1e-5)
                emit_rep(r_m, rep_a[:], 2, TILE)
                emit_rep(r_rstd, rep_b[:], 3, TILE)
                def v_xn(v):
                    for c in range(NCH_M):
                        sl = slice(c * TILE, (c + 1) * TILE)
                        v.tensor_tensor(xn_t[:, sl], x_t[:, sl], rep_a[:], ALU.subtract)
                        v.tensor_tensor(xn_t[:, sl], xn_t[:, sl], rep_b[:], ALU.mult)
                        i = v.tensor_copy(xnb_t[:, sl], xn_t[:, sl])
                    return i
                st("vector", v_xn)
                for c in range(NCH_M):
                    st("scalar", lambda s, c=c: s.activation(
                        sq_t[:], xn_t[:, c * TILE:(c + 1) * TILE], AF.Square))
                    st("tensor", lambda t, c=c: t.matmul(
                        pb[4][0:1, 0:TILE], c_ones_colb[:], sq_t[:],
                        start=(c == 0), stop=(c == NCH_M - 1)))
                st("scalar", lambda s: s.activation(r_msq, pb[4][0:1, 0:TILE],
                                                    AF.Identity, scale=1.0 / DM))
                emit_rsqrt(r_msq, r_rstd, r_t1, r_t2, 1e-6)
                emit_rep(r_rstd, rep_a[:], 5, TILE)
                def v_xn2(v):
                    for c in range(NCH_M):
                        sl = slice(c * TILE, (c + 1) * TILE)
                        i = v.scalar_tensor_tensor(xn2_t[:, sl], xn_t[:, sl],
                                                   c_nsc[:, c:c + 1], rep_a[:],
                                                   ALU.mult, ALU.mult)
                    return i
                st("vector", v_xn2)
                dst("sync",
                    [(S_XN[c * 128:(c + 1) * 128, t0:t0 + TILE],
                      xn_t[:, c * TILE:(c + 1) * TILE]) for c in range(NCH_M)] +
                    [(S_XNBF[c * 128:(c + 1) * 128, t0:t0 + TILE],
                      xnb_t[:, c * TILE:(c + 1) * TILE]) for c in range(NCH_M)] +
                    [(S_XN2BF[c * 128:(c + 1) * 128, t0:t0 + TILE],
                      xn2_t[:, c * TILE:(c + 1) * TILE]) for c in range(NCH_M)])

        if max_stage >= 1:
            with ExitStack() as s1:
                e1 = s1.enter_context
                B.stage(p1)

        def p2():
            w_win = e1(nc.sbuf_tensor("w_win", [128, NCH_M * 2 * DI], BF16))
            xn2_t = e1(nc.sbuf_tensor("p2xn2", [128, NCH_M * TILE], BF16))
            u_t = e1(nc.sbuf_tensor("p2u", [128, NCH_I * TILE], BF16))
            z_t = e1(nc.sbuf_tensor("p2z", [128, NCH_I * TILE], BF16))
            uext = [e1(nc.sbuf_tensor(f"p2uext{i}", [128, TILE + 4], F32))
                    for i in range(2)]
            uc_f = [e1(nc.sbuf_tensor(f"p2uc{i}", [128, TILE], F32)) for i in range(2)]
            ub_f = [e1(nc.sbuf_tensor(f"p2ub{i}", [128, TILE], F32)) for i in range(2)]
            tt_t = [e1(nc.sbuf_tensor(f"p2tt{i}", [128, TILE], F32)) for i in range(2)]
            t2_t = [e1(nc.sbuf_tensor(f"p2t2{i}", [128, TILE], F32)) for i in range(2)]

            m_load = dst("gpsimd",
                         [(w_win[:].rearrange("p (c n) -> p c n", c=NCH_M),
                           W_IN.rearrange("(c p) n -> p c n", p=128))])
            m_store = None
            m_lastmm = None
            m_lastV = None
            for it in range(NTL):
                t0 = it * TILE
                m_xn2 = dst("sync", [(xn2_t[:, c * TILE:(c + 1) * TILE],
                                      S_XN2BF[c * 128:(c + 1) * 128, t0:t0 + TILE])
                                     for c in range(NCH_M)],
                            after=[m_lastmm])
                m_mm, m_ev, m_cv, m_sl, m_fin = {}, {}, {}, {}, {}
                for j in range(2 * NCH_I):
                    par = j % 2
                    u_path = j < NCH_I

                    def mm(t, j=j, par=par):
                        for k in range(NCH_M):
                            i = t.matmul(pb[par][:, 0:TILE],
                                         w_win[:, k * 2 * DI + j * 128:
                                               k * 2 * DI + (j + 1) * 128],
                                         xn2_t[:, k * TILE:(k + 1) * TILE],
                                         start=(k == 0), stop=(k == NCH_M - 1))
                        return i
                    m_mm[j] = st("tensor", mm,
                                 after=[m_xn2, m_load, m_ev.get(j - 2)])
                    if u_path:
                        m_ev[j] = st("scalar", lambda s, j=j, par=par: s.activation(
                            uext[par][:, 3:3 + TILE], pb[par][:, 0:TILE],
                            AF.Identity, bias=c_bin[:, j:j + 1]),
                            after=[m_mm[j], m_cv.get(j - 2)])

                        def v_conv(v, j=j, par=par):
                            v.tensor_copy(uext[par][:, 0:3],
                                          cv_hist[:, j * 4:j * 4 + 3])
                            v.tensor_scalar_mul(uc_f[par][:], uext[par][:, 0:TILE],
                                                c_convw[:, j * 4:j * 4 + 1])
                            for k in range(1, 4):
                                v.scalar_tensor_tensor(
                                    uc_f[par][:], uext[par][:, k:k + TILE],
                                    c_convw[:, j * 4 + k:j * 4 + k + 1],
                                    uc_f[par][:], ALU.mult, ALU.add)
                            return v.tensor_copy(cv_hist[:, j * 4:j * 4 + 3],
                                                 uext[par][:, TILE:TILE + 3])
                        m_cv[j] = st("vector", v_conv,
                                     after=[m_ev[j], m_store])

                        def s_silu(s, j=j, par=par):
                            s.activation(ub_f[par][:], uc_f[par][:], AF.Identity,
                                         bias=c_convb[:, j:j + 1])
                            return s.activation(tt_t[par][:], ub_f[par][:],
                                               AF.Tanh, scale=0.5)
                        m_sl[j] = st("scalar", s_silu,
                                     after=[m_cv[j], m_fin.get(j - 2)])

                        def v_fin(v, j=j, par=par):
                            v.tensor_scalar(t2_t[par][:], tt_t[par][:], 1.0, 0.5,
                                            ALU.add, ALU.mult)
                            return v.tensor_tensor(
                                u_t[:, j * TILE:(j + 1) * TILE],
                                ub_f[par][:], t2_t[par][:], ALU.mult)
                        m_fin[j] = st("vector", v_fin, after=[m_sl[j]])
                    else:
                        jz = j - NCH_I
                        def s_evz(s, j=j, par=par):
                            s.activation(ub_f[par][:], pb[par][:, 0:TILE],
                                         AF.Identity, bias=c_bin[:, j:j + 1])
                            return s.activation(tt_t[par][:], ub_f[par][:],
                                               AF.Tanh, scale=0.5)
                        m_ev[j] = st("scalar", s_evz,
                                     after=[m_mm[j], m_fin.get(j - 2)])
                        m_cv[j] = m_ev[j]

                        def v_finz(v, jz=jz, par=par):
                            v.tensor_scalar(t2_t[par][:], tt_t[par][:], 1.0, 0.5,
                                            ALU.add, ALU.mult)
                            return v.tensor_tensor(
                                z_t[:, jz * TILE:(jz + 1) * TILE],
                                ub_f[par][:], t2_t[par][:], ALU.mult)
                        m_fin[j] = st("vector", v_finz, after=[m_ev[j]])
                m_lastmm = m_mm[2 * NCH_I - 1]
                m_lastV = m_fin[2 * NCH_I - 1]
                m_store = dst("sync",
                    [(S_UBF[c * 128:(c + 1) * 128, t0:t0 + TILE],
                      u_t[:, c * TILE:(c + 1) * TILE]) for c in range(NCH_I)] +
                    [(S_ZBF[c * 128:(c + 1) * 128, t0:t0 + TILE],
                      z_t[:, c * TILE:(c + 1) * TILE]) for c in range(NCH_I)],
                    after=[m_lastV])
            B.prev_mark = m_store

        if max_stage >= 2:
            with ExitStack() as s1:
                e1 = s1.enter_context
                B.stage(p2)

        def p3():
            u_t = e1(nc.sbuf_tensor("p3u", [128, NCH_I * TILE], BF16))
            dtr_t = e1(nc.sbuf_tensor("p3dtr", [DR, TILE], BF16))
            dt_t = e1(nc.sbuf_tensor("p3dt", [128, NCH_I * TILE], BF16))
            yy = e1(nc.sbuf_tensor("p3y", [128, TILE], F32))
            q1 = e1(nc.sbuf_tensor("p3q1", [128, TILE], F32))
            ba_t = e1(nc.sbuf_tensor("p3b", [DS, TILE], F32))
            ca_t = e1(nc.sbuf_tensor("p3c", [DS, TILE], F32))

            for it in range(NTL):
                t0 = it * TILE
                dst("sync", [(u_t[:, c * TILE:(c + 1) * TILE],
                              S_UBF[c * 128:(c + 1) * 128, t0:t0 + TILE])
                             for c in range(NCH_I)])
                def mm_dtr(t):
                    for k in range(NCH_I):
                        i = t.matmul(pb[0][0:DR, 0:TILE],
                                     c_wdtin[:, k * DR:(k + 1) * DR],
                                     u_t[:, k * TILE:(k + 1) * TILE],
                                     start=(k == 0), stop=(k == NCH_I - 1))
                    return i
                st("tensor", mm_dtr)
                st("scalar", lambda s: s.activation(dtr_t[:], pb[0][0:DR, 0:TILE],
                                                    AF.Identity,
                                                    bias=c_bdtin[:, 0:1]))
                def mm_b(t):
                    for k in range(NCH_I):
                        i = t.matmul(pb[1][0:DS, 0:TILE],
                                     c_wb[:, k * DS:(k + 1) * DS],
                                     u_t[:, k * TILE:(k + 1) * TILE],
                                     start=(k == 0), stop=(k == NCH_I - 1))
                    return i
                st("tensor", mm_b)
                st("scalar", lambda s: s.activation(ba_t[:], pb[1][0:DS, 0:TILE],
                                                    AF.Tanh, bias=c_bb[:, 0:1]))
                def mm_c(t):
                    for k in range(NCH_I):
                        i = t.matmul(pb[2][0:DS, 0:TILE],
                                     c_wc[:, k * DS:(k + 1) * DS],
                                     u_t[:, k * TILE:(k + 1) * TILE],
                                     start=(k == 0), stop=(k == NCH_I - 1))
                    return i
                st("tensor", mm_c)
                st("scalar", lambda s: s.activation(ca_t[:], pb[2][0:DS, 0:TILE],
                                                    AF.Tanh, bias=c_bc[:, 0:1]))
                dst("sync", [(S_B[:, t0:t0 + TILE], ba_t[:]),
                             (S_C[:, t0:t0 + TILE], ca_t[:])])
                for j in range(NCH_I):
                    st("tensor", lambda t, j=j: t.matmul(
                        pb[3][:, 0:TILE], c_wdtout[:, j * 128:(j + 1) * 128],
                        dtr_t[:], start=True, stop=True))
                    st("scalar", lambda s, j=j: s.activation(
                        yy[:], pb[3][:, 0:TILE], AF.Exp,
                        bias=c_bdtsum[:, j:j + 1]))
                    def v_dt(v, j=j):
                        v.tensor_scalar(q1[:], yy[:], 1.0 / 3.0, -0.5,
                                        ALU.mult, ALU.add)
                        v.tensor_tensor(q1[:], q1[:], yy[:], ALU.mult)
                        v.tensor_scalar_add(q1[:], q1[:], 1.0)
                        v.tensor_tensor(q1[:], q1[:], yy[:], ALU.mult)
                        return v.tensor_scalar(dt_t[:, j * TILE:(j + 1) * TILE],
                                               q1[:], 1e-3, 0.1, ALU.max, ALU.min)
                    st("vector", v_dt)
                dst("sync", [(S_DTBF[c * 128:(c + 1) * 128, t0:t0 + TILE],
                              dt_t[:, c * TILE:(c + 1) * TILE])
                             for c in range(NCH_I)])

        if max_stage >= 3:
            with ExitStack() as s1:
                e1 = s1.enter_context
                B.stage(p3)

        def p4():
            u_t = e1(nc.sbuf_tensor("p4u", [128, NCH_I * TILE], BF16))
            z_t = e1(nc.sbuf_tensor("p4z", [128, NCH_I * TILE], BF16))
            dt_t = e1(nc.sbuf_tensor("p4dt", [128, NCH_I * TILE], BF16))
            y_t = e1(nc.sbuf_tensor("p4y", [128, NCH_I * TILE], BF16))
            ba_t = e1(nc.sbuf_tensor("p4b", [DS, TILE], F32))
            ca_t = e1(nc.sbuf_tensor("p4c", [DS, TILE], F32))
            brep = [e1(nc.sbuf_tensor(f"p4brep{i}", [128, 8 * TILE], BF16))
                    for i in range(2)]
            crep = [e1(nc.sbuf_tensor(f"p4crep{i}", [128, 8 * TILE], BF16))
                    for i in range(2)]
            da_t = [e1(nc.sbuf_tensor(f"p4da{i}", [128, 8 * TILE], F32))
                    for i in range(2)]
            db_t = e1(nc.sbuf_tensor("p4db", [128, 8 * TILE], BF16))
            h_t = e1(nc.sbuf_tensor("p4h", [128, 8 * TILE], BF16))
            m_t = e1(nc.sbuf_tensor("p4m", [128, 8 * TILE], BF16))
            tr4 = e1(nc.sbuf_tensor("p4tr4", [128, 4 * TILE], BF16))
            dtu_t = e1(nc.sbuf_tensor("p4dtu", [128, TILE], BF16))
            ys_t = e1(nc.sbuf_tensor("p4ys", [128, TILE], F32))
            yh_t = e1(nc.sbuf_tensor("p4yh", [128, TILE], F32))

            m_store = None
            m_lastS = None
            m_lastV = None
            for it in range(NTL):
                t0 = it * TILE
                m_ld = dst("sync",
                    [(u_t[:, c * TILE:(c + 1) * TILE],
                      S_UBF[c * 128:(c + 1) * 128, t0:t0 + TILE]) for c in range(NCH_I)] +
                    [(z_t[:, c * TILE:(c + 1) * TILE],
                      S_ZBF[c * 128:(c + 1) * 128, t0:t0 + TILE]) for c in range(NCH_I)] +
                    [(dt_t[:, c * TILE:(c + 1) * TILE],
                      S_DTBF[c * 128:(c + 1) * 128, t0:t0 + TILE]) for c in range(NCH_I)] +
                    [(ba_t[:], S_B[:, t0:t0 + TILE]),
                     (ca_t[:], S_C[:, t0:t0 + TILE])],
                    after=[m_lastS, m_lastV, m_store])
                m_da, m_scan = {}, {}
                m_rep = {}
                for hf in range(2):
                    mr = dst("gpsimd", [(brep[hf][0:1, :], ba_t[hf * 8:hf * 8 + 8, :]),
                                        (crep[hf][0:1, :], ca_t[hf * 8:hf * 8 + 8, :])],
                             after=[m_ld, m_lastV])
                    for kd in range(7):
                        mr = dst("sync",
                                 [(brep[hf][1 << kd:2 << kd, :], brep[hf][0:1 << kd, :]),
                                  (crep[hf][1 << kd:2 << kd, :], crep[hf][0:1 << kd, :])],
                                 after=[mr])
                    m_rep[hf] = mr
                for c in range(NCH_I):
                    for hf in range(2):
                        un = c * 2 + hf
                        par = un % 2

                        def s_da(s, c=c, hf=hf, par=par):
                            for sl in range(8):
                                i = s.activation(
                                    da_t[par][:, sl * TILE:(sl + 1) * TILE],
                                    dt_t[:, c * TILE:(c + 1) * TILE], AF.Exp,
                                    scale=c_a[:, c * DS + hf * 8 + sl:
                                              c * DS + hf * 8 + sl + 1])
                            return i
                        m_da[un] = st("scalar", s_da,
                                      after=[m_ld, m_scan.get(un - 2)])

                        def v_pre(v, c=c, hf=hf):
                            v.tensor_tensor(dtu_t[:],
                                            dt_t[:, c * TILE:(c + 1) * TILE],
                                            u_t[:, c * TILE:(c + 1) * TILE],
                                            ALU.mult)
                            b3v = brep[hf][:].rearrange("p (s t) -> p s t", s=8)
                            db3 = db_t[:].rearrange("p (s t) -> p s t", s=8)
                            u3 = dtu_t[:].unsqueeze(1).broadcast_to([128, 8, TILE])
                            return v.tensor_tensor(db3, b3v, u3, ALU.mult)
                        mpre = st("vector", v_pre,
                                  after=[m_ld, m_rep[hf]] if c == 0 else [m_ld])

                        def v_scan(v, c=c, hf=hf, par=par):
                            for sl in range(8):
                                v.tensor_tensor_scan(
                                    h_t[:, sl * TILE:(sl + 1) * TILE],
                                    da_t[par][:, sl * TILE:(sl + 1) * TILE],
                                    db_t[:, sl * TILE:(sl + 1) * TILE],
                                    sc_state[:, c * DS + hf * 8 + sl:
                                             c * DS + hf * 8 + sl + 1],
                                    ALU.mult, ALU.add)
                            v.tensor_copy(
                                sc_state[:, c * DS + hf * 8:c * DS + hf * 8 + 8],
                                h_t[:].rearrange("p (s t) -> p s t", s=8)[:, :, TILE - 1])
                            v.tensor_tensor(m_t[:], h_t[:], crep[hf][:], ALU.mult)
                            v.tensor_tensor(tr4[:], m_t[:, 0:4 * TILE],
                                            m_t[:, 4 * TILE:8 * TILE], ALU.add)
                            v.tensor_tensor(tr4[:, 0:2 * TILE], tr4[:, 0:2 * TILE],
                                            tr4[:, 2 * TILE:4 * TILE], ALU.add)
                            ys_c = ys_t[:]
                            if hf == 0:
                                i = v.tensor_tensor(ys_c, tr4[:, 0:TILE],
                                                    tr4[:, TILE:2 * TILE], ALU.add)
                            else:
                                v.tensor_tensor(yh_t[:], tr4[:, 0:TILE],
                                                tr4[:, TILE:2 * TILE], ALU.add)
                                i = v.tensor_tensor(ys_c, ys_c, yh_t[:], ALU.add)
                            return i
                        m_scan[un] = st("vector", v_scan, after=[mpre, m_da[un]])
                    def v_y(v, c=c):
                        v.scalar_tensor_tensor(yh_t[:],
                                               u_t[:, c * TILE:(c + 1) * TILE],
                                               c_d[:, c:c + 1], ys_t[:],
                                               ALU.mult, ALU.add)
                        return v.tensor_tensor(y_t[:, c * TILE:(c + 1) * TILE],
                                               yh_t[:],
                                               z_t[:, c * TILE:(c + 1) * TILE],
                                               ALU.mult)
                    mv = st("vector", v_y, after=[m_scan[un]])
                m_lastV = mv
                m_lastS = m_da[2 * NCH_I - 1]
                m_store = dst("sync", [(S_YBF[c * 128:(c + 1) * 128, t0:t0 + TILE],
                                        y_t[:, c * TILE:(c + 1) * TILE])
                                       for c in range(NCH_I)],
                              after=[m_lastV])
            B.prev_mark = m_store

        if max_stage >= 4:
            with ExitStack() as s1:
                e1 = s1.enter_context
                B.stage(p4)

        def p5():
            w_wo = e1(nc.sbuf_tensor("p5wo", [128, NCH_I * DM], BF16))
            y_t = e1(nc.sbuf_tensor("p5y", [128, NCH_I * TILE], BF16))
            xn_t = e1(nc.sbuf_tensor("p5xn", [128, NCH_M * TILE], F32))
            s_t = e1(nc.sbuf_tensor("p5s", [128, NCH_M * TILE], F32))

            dst("gpsimd", [(w_wo[:].rearrange("p (c n) -> p c n", c=NCH_I),
                            W_OUT.rearrange("(c p) n -> p c n", p=128))])
            for it in range(NTL):
                t0 = it * TILE
                dst("sync",
                    [(y_t[:, c * TILE:(c + 1) * TILE],
                      S_YBF[c * 128:(c + 1) * 128, t0:t0 + TILE]) for c in range(NCH_I)] +
                    [(xn_t[:, c * TILE:(c + 1) * TILE],
                      S_XN[c * 128:(c + 1) * 128, t0:t0 + TILE]) for c in range(NCH_M)])
                for j in range(NCH_M):
                    def mm(t, j=j):
                        for k in range(NCH_I):
                            i = t.matmul(pb[0][:, 0:TILE],
                                         w_wo[:, k * DM + j * 128:k * DM + (j + 1) * 128],
                                         y_t[:, k * TILE:(k + 1) * TILE],
                                         start=(k == 0), stop=(k == NCH_I - 1))
                        return i
                    st("tensor", mm)
                    st("vector", lambda v, j=j: v.scalar_tensor_tensor(
                        s_t[:, j * TILE:(j + 1) * TILE], pb[0][:, 0:TILE],
                        c_bout[:, j:j + 1], xn_t[:, j * TILE:(j + 1) * TILE],
                        ALU.add, ALU.add))
                dst("sync", [(S_S[c * 128:(c + 1) * 128, t0:t0 + TILE],
                              s_t[:, c * TILE:(c + 1) * TILE])
                             for c in range(NCH_M)])

        if max_stage >= 5:
            with ExitStack() as s1:
                e1 = s1.enter_context
                B.stage(p5)

        def p6():
            P6STOP = int(os.environ.get("P6STOP", "99"))
            w1s2 = [e1(nc.sbuf_tensor(f"p6w1_{i}", [128, NCH_M * DM], BF16))
                    for i in range(2)]
            w3s2 = [e1(nc.sbuf_tensor(f"p6w3_{i}", [128, NCH_M * DM], BF16))
                    for i in range(2)]
            w2s = e1(nc.sbuf_tensor("p6w2", [128, NCH_M * DM], BF16))
            xnb_t = e1(nc.sbuf_tensor("p6xnb", [128, NCH_M * FT], BF16))
            hb_t = e1(nc.sbuf_tensor("p6hb", [128, NCH_M * FT], BF16))
            yac_t = e1(nc.sbuf_tensor("p6yac", [128, NCH_M * FT], F32))
            p1b = [e1(nc.sbuf_tensor(f"p6p1b{i}", [128, FT], BF16)) for i in range(2)]
            p3b = [e1(nc.sbuf_tensor(f"p6p3b{i}", [128, FT], BF16)) for i in range(2)]
            th_t = [e1(nc.sbuf_tensor(f"p6th{i}", [128, FT], BF16)) for i in range(2)]
            t2_t = e1(nc.sbuf_tensor("p6t2", [128, FT], BF16))
            crepe = [e1(nc.sbuf_tensor(f"p6crepe{i}", [128, FT], BF16))
                     for i in range(2)]
            cfm16 = e1(nc.sbuf_tensor("p6cfm16", [128, FT], BF16))
            glog = e1(nc.sbuf_tensor("p6glog", [128, NQ * NE], F32))
            gprob = e1(nc.sbuf_tensor("p6gprob", [128, NQ * NE], F32))
            gsel = e1(nc.sbuf_tensor("p6gsel", [128, NQ * NE], F32))
            gw8 = e1(nc.sbuf_tensor("p6gw8", [128, NQ * NE], F32))
            gred = e1(nc.sbuf_tensor("p6gred", [128, 2 * NQ], F32))
            gq8all = e1(nc.sbuf_tensor("p6gq8all", [128, NQ * NE], BF16))
            xc_t = e1(nc.sbuf_tensor("p6xc", [128, FT], F32))
            sc_c = e1(nc.sbuf_tensor("p6sc", [128, 512], F32))
            oc_t = e1(nc.sbuf_tensor("p6oc", [128, 512], F32))
            gth = e1(nc.sbuf_tensor("p6gth", [128, 512], F32))


            m_prevT = None
            m_prevV = None
            m_store = None
            m_w13 = {}
            m_w2l = {}
            for ft in range(NFT):
                t0 = ft * FT

                def wpairs(ex):
                    if ex == NE:
                        return [(w1s2[ex % 2][:].rearrange("p (c n) -> p c n", c=NCH_M),
                                 SW1.rearrange("(c p) n -> p c n", p=128)),
                                (w3s2[ex % 2][:].rearrange("p (c n) -> p c n", c=NCH_M),
                                 SW3.rearrange("(c p) n -> p c n", p=128))]
                    return [(w1s2[ex % 2][:].rearrange("p (c n) -> p c n", c=NCH_M),
                             W1[ex * DM:(ex + 1) * DM, :].rearrange("(c p) n -> p c n", p=128)),
                            (w3s2[ex % 2][:].rearrange("p (c n) -> p c n", c=NCH_M),
                             W3[ex * DM:(ex + 1) * DM, :].rearrange("(c p) n -> p c n", p=128))]

                def w2pair(ex):
                    if ex == NE:
                        return [(w2s[:].rearrange("p (c n) -> p c n", c=NCH_M),
                                 SW2.rearrange("(c p) n -> p c n", p=128))]
                    return [(w2s[:].rearrange("p (c n) -> p c n", c=NCH_M),
                             W2[ex * DM:(ex + 1) * DM, :].rearrange("(c p) n -> p c n", p=128))]

                m_xnb = dst("sync",
                            [(xnb_t[:, c * FT:(c + 1) * FT],
                              S_XNBF[c * 128:(c + 1) * 128, t0:t0 + FT])
                             for c in range(NCH_M)],
                            after=[m_prevT, m_prevV, m_store])
                m_w13[0] = dst("gpsimd", wpairs(0), after=[m_prevT, m_xnb])
                m_w2l[0] = dst("gpsimd", w2pair(0), after=[m_prevT, m_w13[0]])
                B.prev_mark = m_xnb
                for k in range(NCH_M):
                    dst("sync", [(xc_t[:], S_XN[k * 128:(k + 1) * 128, t0:t0 + FT])],
                        after=[B.prev_mark, m_prevV, m_prevT])

                    def mm_g(t, k=k):
                        for q in range(NQ):
                            i = t.matmul(pb[q][:, 0:NE],
                                         xc_t[:, q * 128:(q + 1) * 128],
                                         c_gate[:, k * NE:(k + 1) * NE],
                                         start=(k == 0), stop=(k == NCH_M - 1))
                        return i
                    st("tensor", mm_g)
                def v_glog(v):
                    for q in range(NQ):
                        i = v.tensor_copy(glog[:, q * NE:(q + 1) * NE],
                                          pb[q][:, 0:NE])
                    return i
                st("vector", v_glog)
                if P6STOP <= 1:
                    m_prevT = B.prev_mark; m_prevV = B.prev_mark
                    m_store = B.prev_mark
                    continue
                g3 = glog[:].rearrange("p (q e) -> p q e", e=NE)
                st("vector", lambda v: v.tensor_reduce(
                    gred[:, 0:NQ], g3, axis=AXL.X, op=ALU.max))
                st("vector", lambda v: v.tensor_scalar_mul(
                    gred[:, NQ:2 * NQ], gred[:, 0:NQ], -1.0))
                def v_sub(v):
                    for q in range(NQ):
                        i = v.tensor_scalar_add(gprob[:, q * NE:(q + 1) * NE],
                                                glog[:, q * NE:(q + 1) * NE],
                                                gred[:, NQ + q:NQ + q + 1])
                    return i
                st("vector", v_sub)
                st("scalar", lambda s: s.activation(gprob[:], gprob[:], AF.Exp))
                st("vector", lambda v: v.tensor_reduce(
                    gred[:, 0:NQ], gprob[:].rearrange("p (q e) -> p q e", e=NE),
                    axis=AXL.X, op=ALU.add))
                st("vector", lambda v: v.reciprocal(gred[:, NQ:2 * NQ],
                                                    gred[:, 0:NQ]))
                def v_nrm(v):
                    for q in range(NQ):
                        i = v.tensor_scalar_mul(gprob[:, q * NE:(q + 1) * NE],
                                                gprob[:, q * NE:(q + 1) * NE],
                                                gred[:, NQ + q:NQ + q + 1])
                    return i
                st("vector", v_nrm)
                if P6STOP <= 2:
                    m_prevT = B.prev_mark; m_prevV = B.prev_mark
                    m_store = B.prev_mark
                    continue
                def v_w8(v):
                    for q in range(NQ):
                        i = v.max(gw8[:, q * NE:(q + 1) * NE],
                                  glog[:, q * NE:(q + 1) * NE])
                    return i
                st("vector", v_w8)
                def v_sel(v):
                    for q in range(NQ):
                        i = v.tensor_scalar(gsel[:, q * NE:(q + 1) * NE],
                                            glog[:, q * NE:(q + 1) * NE],
                                            gw8[:, q * NE + 1:q * NE + 2], None,
                                            ALU.is_ge)
                    return i
                st("vector", v_sel)
                st("vector", lambda v: v.tensor_tensor(gprob[:], gprob[:], gsel[:],
                                                       ALU.mult))
                st("vector", lambda v: v.tensor_copy(gq8all[:], gprob[:]))
                if P6STOP <= 3:
                    m_prevT = B.prev_mark; m_prevV = B.prev_mark
                    m_store = B.prev_mark
                    continue
                def t_tr(t):
                    if os.environ.get("TRNOP"):
                        return t.nop()
                    for q in range(NQ):
                        i = t.transpose(pb[q][0:NE, 0:64].bitcast(BF16),
                                        gq8all[:, q * NE:(q + 1) * NE], c_ident[:])
                    return i
                st("tensor", t_tr)
                def s_trev(s):
                    for q in range(NQ):
                        i = s.activation(cfm16[0:NE, q * 128:(q + 1) * 128],
                                         pb[q][0:NE, 0:64].bitcast(BF16), AF.Identity)
                    return i
                m_cfm = st("scalar", s_trev)
                if ft == 0:
                    m_cfm = dst("sync", [(cfm16[NE:128, :], onespad_dram.ap())])
                m_gate = m_cfm

                if P6STOP <= 4:
                    m_prevT = B.prev_mark; m_prevV = B.prev_mark
                    m_store = B.prev_mark
                    continue
                m_p13 = {}
                m_sev = {}
                m_vh = {}
                m_vh_g = {}
                m_w2mm = {}
                m_vacc = {}
                m_crepe = {}
                NEX = min(NE + 1, int(os.environ.get("P6LIM", "99")))
                for ex in range(NEX):
                    shared = (ex == NE)
                    exp = ex % 2
                    if ex + 1 <= NE and ex + 1 < NEX:
                        m_w13[ex + 1] = dst("gpsimd", wpairs(ex + 1),
                                            after=[m_p13.get((ex - 1, NCH_M - 1)),
                                                   m_w13.get(ex)])
                    if not shared:
                        mr = dst("sync", [(crepe[exp][0:1, :],
                                           cfm16[ex:ex + 1, :])],
                                 after=[m_cfm, m_vh.get((ex - 2, NCH_M - 1))])
                        for kd in range(7):
                            mr = dst("sync", [(crepe[exp][1 << kd:2 << kd, :],
                                               crepe[exp][0:1 << kd, :])], after=[mr])
                        m_crepe[ex] = mr
                    b1c = c_sb1 if shared else c_b1[:, ex * NCH_M:(ex + 1) * NCH_M]
                    b3c = c_sb3 if shared else c_b3[:, ex * NCH_M:(ex + 1) * NCH_M]
                    for j in range(NCH_M):
                        jp = j % 2
                        bb = jp * 4

                        def mm_p13(t, j=j, exp=exp, bb=bb):
                            for hh in range(2):
                                for k in range(NCH_M):
                                    t.matmul(pb[bb + hh][:, 0:512],
                                             w1s2[exp][:, k * DM + j * 128:k * DM + (j + 1) * 128],
                                             xnb_t[:, k * FT + hh * 512:k * FT + (hh + 1) * 512],
                                             start=(k == 0), stop=(k == NCH_M - 1))
                                for k in range(NCH_M):
                                    i = t.matmul(pb[bb + 2 + hh][:, 0:512],
                                                 w3s2[exp][:, k * DM + j * 128:k * DM + (j + 1) * 128],
                                                 xnb_t[:, k * FT + hh * 512:k * FT + (hh + 1) * 512],
                                                 start=(k == 0), stop=(k == NCH_M - 1))
                            return i
                        deps = [m_w13[ex], m_sev.get((ex, j - 2))]
                        if j == 0:
                            deps += [m_gate,
                                     m_sev.get((ex - 1, NCH_M - 1)),
                                     m_vacc.get((ex - 1, NCH_M - 1))]
                        m_p13[(ex, j)] = st("tensor", mm_p13, after=deps)

                        def s_ev(s, j=j, jp=jp, bb=bb, b1c=b1c, b3c=b3c):
                            for hh in range(2):
                                s.activation(p1b[jp][:, hh * 512:(hh + 1) * 512],
                                             pb[bb + hh][:, 0:512], AF.Identity,
                                             bias=b1c[:, j:j + 1])
                                s.activation(p3b[jp][:, hh * 512:(hh + 1) * 512],
                                             pb[bb + 2 + hh][:, 0:512], AF.Identity,
                                             bias=b3c[:, j:j + 1])
                            return s.activation(th_t[jp][:], p1b[jp][:],
                                               AF.Tanh, scale=0.5)
                        gu = ex * NCH_M + j
                        m_sev[(ex, j)] = st("scalar", s_ev,
                                            after=[m_p13[(ex, j)],
                                                   m_vh_g.get(gu - 2)])

                        def v_h(v, j=j, jp=jp, shared=shared, exp=exp):
                            v.tensor_scalar(t2_t[:], th_t[jp][:], 1.0, 0.5,
                                            ALU.add, ALU.mult)
                            v.tensor_tensor(t2_t[:], t2_t[:], p1b[jp][:], ALU.mult)
                            v.tensor_tensor(t2_t[:], t2_t[:], p3b[jp][:], ALU.mult)
                            if shared:
                                return v.tensor_copy(hb_t[:, j * FT:(j + 1) * FT],
                                                     t2_t[:])
                            return v.tensor_tensor(hb_t[:, j * FT:(j + 1) * FT],
                                                   t2_t[:], crepe[exp][:], ALU.mult)
                        vdeps = [m_sev[(ex, j)]]
                        if not shared and j == 0:
                            vdeps.append(m_crepe[ex])
                        m_vh[(ex, j)] = st("vector", v_h, after=vdeps)
                        m_vh_g[gu] = m_vh[(ex, j)]
                    for j2 in range(NCH_M):
                        j2p = j2 % 2
                        bb = j2p * 2

                        def mm_w2(t, j2=j2, ex=ex, shared=shared, bb=bb):
                            for hh in range(2):
                                for k in range(NCH_M):
                                    i = t.matmul(pb[bb + hh][:, 0:512],
                                                 w2s[:, k * DM + j2 * 128:k * DM + (j2 + 1) * 128],
                                                 hb_t[:, k * FT + hh * 512:k * FT + (hh + 1) * 512],
                                                 start=(k == 0), stop=(k == NCH_M - 1))
                            return i
                        deps = [m_vh[(ex, NCH_M - 1)], m_w2l[ex],
                                m_vacc.get((ex, j2 - 2))]
                        if j2 == 0:
                            deps.append(m_sev[(ex, NCH_M - 1)])
                        m_w2mm[(ex, j2)] = st("tensor", mm_w2, after=deps)

                        def v_acc(v, j2=j2, ex=ex, bb=bb):
                            for hh in range(2):
                                dst_sl = yac_t[:, j2 * FT + hh * 512:j2 * FT + (hh + 1) * 512]
                                if ex == 0:
                                    i = v.tensor_copy(dst_sl, pb[bb + hh][:, 0:512])
                                else:
                                    i = v.tensor_tensor(dst_sl, dst_sl,
                                                        pb[bb + hh][:, 0:512], ALU.add)
                            return i
                        m_vacc[(ex, j2)] = st("vector", v_acc,
                                              after=[m_w2mm[(ex, j2)]])
                    if ex + 1 <= NE and ex + 1 < NEX:
                        m_w2l[ex + 1] = dst("gpsimd", w2pair(ex + 1),
                                            after=[m_w2mm[(ex, NCH_M - 1)]])
                if NEX < NE + 1:
                    m_vacc[(NE, NCH_M - 1)] = m_vacc.get((NEX - 1, NCH_M - 1)) if NEX > 0 else m_gate
                    for _j2 in range(NCH_M):
                        m_vacc[(NE, _j2)] = m_vacc[(NE, NCH_M - 1)]
                m_ybias = {}
                for j2 in range(NCH_M):
                    j2p = j2 % 2
                    bb = j2p * 2

                    def mm_yb(t, j2=j2, bb=bb):
                        import os as _os
                        rhs_t = xnb_t if _os.environ.get("YBRHS") else cfm16
                        lhs_t = (w2s[:, 0:DM] if _os.environ.get("YBLHS")
                                 else c_b2aug)
                        for hh in range(2):
                            i = t.matmul(pb[bb + hh][:, 0:512],
                                         lhs_t[:, j2 * 128:(j2 + 1) * 128],
                                         rhs_t[:, hh * 512:(hh + 1) * 512],
                                         start=True, stop=True)
                        return i
                    m_ybias[j2] = st("tensor", mm_yb,
                                     after=[m_vacc[(NE, j2)], m_ybias.get(j2 - 2),
                                            m_vacc.get((NE + 1, j2 - 2)),
                                            m_vacc.get((NE, NCH_M - 1)) if j2 < 2 else None])

                    def v_accb(v, j2=j2, bb=bb):
                        if os.environ.get("VACCNOP"):
                            return v.engine_nop()
                        for hh in range(2):
                            dst_sl = yac_t[:, j2 * FT + hh * 512:j2 * FT + (hh + 1) * 512]
                            i = v.tensor_tensor(dst_sl, dst_sl,
                                                pb[bb + hh][:, 0:512], ALU.add)
                        return i
                    m_vacc[(NE + 1, j2)] = st("vector", v_accb, after=[m_ybias[j2]])
                m_prevT = m_ybias[NCH_M - 1]
                if P6STOP <= 5:
                    m_prevT = B.prev_mark; m_prevV = B.prev_mark
                    m_store = B.prev_mark
                    continue
                B.prev_mark = m_vacc[(NE + 1, NCH_M - 1)]
                for j2 in range(NCH_M):
                    for hh in range(2):
                        sl = slice(j2 * FT + hh * 512, j2 * FT + (hh + 1) * 512)
                        hsl = slice(t0 + hh * 512, t0 + (hh + 1) * 512)
                        dst("sync", [(xc_t[:, 0:512], X[j2 * 128:(j2 + 1) * 128, hsl]),
                                     (sc_c[:], S_S[j2 * 128:(j2 + 1) * 128, hsl])])
                        st("scalar", lambda s, sl=sl: s.activation(
                            gth[:], yac_t[:, sl], AF.Tanh, scale=0.5))
                        def v_out(v, sl=sl):
                            v.tensor_scalar(gth[:], gth[:], 1.0, 0.5, ALU.add, ALU.mult)
                            v.tensor_tensor(oc_t[:], yac_t[:, sl], gth[:], ALU.mult)
                            v.tensor_tensor(oc_t[:], oc_t[:], sc_c[:], ALU.mult)
                            return v.tensor_tensor(oc_t[:], oc_t[:], xc_t[:, 0:512],
                                                   ALU.add)
                        st("vector", v_out)
                        dst("sync", [(OUT[j2 * 128:(j2 + 1) * 128, hsl], oc_t[:])])
                if B.debug:
                    dst("sync", [(D_CFM[:, t0:t0 + FT], cfm16[:])] +
                        [(D_YAC[j2 * 128:(j2 + 1) * 128, t0:t0 + FT],
                          yac_t[:, j2 * FT:(j2 + 1) * FT]) for j2 in range(NCH_M)])
                m_prevV = B.prev_mark
                m_store = B.prev_mark

        if max_stage >= 6:
            with ExitStack() as s1:
                e1 = s1.enter_context
                B.stage(p6)

        def pf():
            st("sync", lambda E: E.nop())
        B.stage(pf)

    return B



def _layout_inputs(inputs):
    g = lambda k: np.asarray(inputs[k])
    out = {}
    x = g("L0_attn_x")
    xT = np.ascontiguousarray(x.transpose(0, 2, 1))

    def chunk_layout(vec, nch):
        return np.ascontiguousarray(np.asarray(vec).reshape(nch, 128).T)

    out["w_in"] = np.ascontiguousarray(g("L0_attn_in_proj_w"))
    out["b_in_l"] = chunk_layout(g("L0_attn_in_proj_b"), 2 * NCH_I)
    out["w_out"] = np.ascontiguousarray(g("L0_attn_out_proj_w"))
    out["b_out_l"] = chunk_layout(g("L0_attn_out_proj_b"), NCH_M)
    out["w_dtin"] = np.ascontiguousarray(g("L0_attn_dt_in_w"))
    out["b_dtin_l"] = np.ascontiguousarray(g("L0_attn_dt_in_b").reshape(DR, 1))
    out["w_dtout"] = np.ascontiguousarray(g("L0_attn_dt_out_w"))
    out["b_dtsum_l"] = chunk_layout(
        g("L0_attn_dt_out_b") + g("L0_attn_dt_bias"), NCH_I)
    out["w_b"] = np.ascontiguousarray(g("L0_attn_b_proj_w"))
    out["b_b_l"] = np.ascontiguousarray(g("L0_attn_b_proj_b").reshape(DS, 1))
    out["w_c"] = np.ascontiguousarray(g("L0_attn_c_proj_w"))
    out["b_c_l"] = np.ascontiguousarray(g("L0_attn_c_proj_b").reshape(DS, 1))
    ck = g("L0_attn_conv_kernel")
    out["convw_l"] = np.ascontiguousarray(
        ck.T.reshape(NCH_I, 128, 4).transpose(1, 0, 2).reshape(128, NCH_I * 4))
    out["convb_l"] = chunk_layout(g("L0_attn_conv_bias"), NCH_I)
    al = g("L0_attn_a_log")
    out["alog_l"] = np.ascontiguousarray(
        al.reshape(NCH_I, 128, DS).transpose(1, 0, 2).reshape(128, NCH_I * DS))
    out["d_l"] = chunk_layout(g("L0_attn_d"), NCH_I)
    out["nsc_l"] = chunk_layout(g("L0_attn_norm_scale"), NCH_M)
    out["gate_wT"] = np.ascontiguousarray(g("L0_ffn_gate_w").T)
    out["ffn_w1"] = np.ascontiguousarray(g("L0_ffn_w1").reshape(NE * DM, DM))
    out["ffn_w2"] = np.ascontiguousarray(g("L0_ffn_w2").reshape(NE * DM, DM))
    out["ffn_w3"] = np.ascontiguousarray(g("L0_ffn_w3").reshape(NE * DM, DM))
    b1 = g("L0_ffn_b1")
    out["ffn_b1_l"] = np.ascontiguousarray(
        b1.reshape(NE, NCH_M, 128).transpose(2, 0, 1).reshape(128, NE * NCH_M))
    b3 = g("L0_ffn_b3")
    out["ffn_b3_l"] = np.ascontiguousarray(
        b3.reshape(NE, NCH_M, 128).transpose(2, 0, 1).reshape(128, NE * NCH_M))
    out["ffn_b2aug"] = np.ascontiguousarray(np.concatenate(
        [g("L0_ffn_b2"), g("L0_ffn_sb2").reshape(1, DM),
         np.zeros((128 - NE - 1, DM), np.float32)], axis=0))
    out["ffn_sw1"] = np.ascontiguousarray(g("L0_ffn_sw1"))
    out["ffn_sw2"] = np.ascontiguousarray(g("L0_ffn_sw2"))
    out["ffn_sw3"] = np.ascontiguousarray(g("L0_ffn_sw3"))
    out["ffn_sb1_l"] = chunk_layout(g("L0_ffn_sb1"), NCH_M)
    out["ffn_sb3_l"] = chunk_layout(g("L0_ffn_sb3"), NCH_M)
    return xT, out


_NC_CACHE = {}


def kernel(**inputs):
    xT, shared = _layout_inputs(inputs)
    if "nc" not in _NC_CACHE:
        _NC_CACHE["nc"] = build(debug=False)
    Bb = _NC_CACHE["nc"]
    in_maps = []
    for b in range(8):
        m = dict(shared)
        m["x_fm"] = np.ascontiguousarray(xT[b])
        in_maps.append(m)
    res = run_bass_kernel_spmd(Bb.nc, in_maps, core_ids=list(range(8)))
    outs = np.stack([np.asarray(r["out_fm"]).T for r in res.results], axis=0)
    return np.ascontiguousarray(outs.astype(np.float32))
